# version 07
# baseline (speedup 1.0000x reference)
import math

import numpy as np
import concourse.bass as bass
import concourse.mybir as mybir
from concourse.bass_utils import run_bass_kernel_spmd

F32 = mybir.dt.float32
I32 = mybir.dt.int32
AF = mybir.ActivationFunctionType
ALU = mybir.AluOpType
AX = mybir.AxisListType

S, DM, OWN, H, KVH, HD = 8192, 2048, 1024, 16, 8, 128
NW, TW = 16, 512
NCH, CH = 8, 1024
NPASS, HPP, KVPP = 4, 4, 2
NQT = 8
HUGE = 1.0e12

SLOPES = [2.0 ** (-(h + 1) / 2.0) for h in range(H)]
B_H = [30.0 - 257.0 * s for s in SLOPES]
SCALE = 1.0 / math.sqrt(float(HD))


def _sp(v):
    hi = float(np.float32(v))
    return hi, float(np.float32(v - hi))


SC_HI, SC_LO = _sp(SCALE)
L64_HI, L64_LO = _sp(-math.log(10000.0) / 64.0)
TWO_PI = 2.0 * math.pi
INV2PI = float(np.float32(1.0 / TWO_PI))
CW1 = 6.28125
CW2 = float(np.float32(TWO_PI - CW1))
CW3 = float(np.float32(TWO_PI - CW1 - CW2))
HPI_HI, HPI_LO = _sp(math.pi / 2.0)
SLOPE_SP = [_sp(s) for s in SLOPES]
FRQ_SP = [_sp(10000.0 ** (-i / 64.0)) for i in range(64)]

ARENA_W = 50000


class Ctr:
    def __init__(self, handle):
        self.h = handle
        self.n = 0

    def inc(self, inst, k=1):
        inst.then_inc(self.h, k)
        self.n += k
        return inst

    def dma(self, inst):
        return self.inc(inst, 16)


class Arena:
    def __init__(self, nc):
        self.t = nc.alloc_sbuf_tensor("arena", [128, ARENA_W], F32)
        self.off = 0
        self.marks = []
        self.peak = 0

    def al(self, parts, n, dtype=F32):
        off = (self.off + 15) & ~15
        self.off = off + n
        self.peak = max(self.peak, self.off)
        assert self.off <= ARENA_W, f"arena overflow: {self.off}"
        ap = self.t[0:parts, off:off + n]
        if dtype is not F32:
            ap = ap.bitcast(dtype)
        return ap

    def push(self):
        self.marks.append(self.off)

    def pop(self):
        self.off = self.marks.pop()


def build_nc(taps=(), stop_after=None, p1_nw=NW, p1_parts=('ln', 'cx', 'tr', 'K', 'V', 'Q')):
    nc = bass.Bass("TRN2", target_bir_lowering=False,
                   detect_race_conditions=False)

    x_d = nc.dram_tensor("x", [S, DM], F32, kind="ExternalInput").ap()
    tokr_d = nc.dram_tensor("tokr", [1, S], F32, kind="ExternalInput").ap()
    tokc_d = nc.dram_tensor("tokc", [128, 64], F32, kind="ExternalInput").ap()
    wq_d = nc.dram_tensor("wq", [DM, DM], F32, kind="ExternalInput").ap()
    wkv_d = nc.dram_tensor("wkv", [DM, DM], F32, kind="ExternalInput").ap()
    wo_d = nc.dram_tensor("wo", [DM, DM], F32, kind="ExternalInput").ap()
    bo_d = nc.dram_tensor("bo", [1, DM], F32, kind="ExternalInput").ap()
    rw_d = nc.dram_tensor("rw", [16, DM], F32, kind="ExternalInput").ap()
    rb_d = nc.dram_tensor("rb", [1, 16], F32, kind="ExternalInput").ap()
    wu_d = nc.dram_tensor("wu", [16, DM, 1024], F32, kind="ExternalInput").ap()
    wg_d = nc.dram_tensor("wg", [16, DM, 1024], F32, kind="ExternalInput").ap()
    wd_d = nc.dram_tensor("wd", [16, 1024, DM], F32, kind="ExternalInput").ap()
    su_d = nc.dram_tensor("su", [DM, DM], F32, kind="ExternalInput").ap()
    sg_d = nc.dram_tensor("sg", [DM, DM], F32, kind="ExternalInput").ap()
    sd_d = nc.dram_tensor("sd", [DM, DM], F32, kind="ExternalInput").ap()
    c_d = nc.dram_tensor("cvec", [1, DM], F32, kind="ExternalInput").ap()
    wmod_d = nc.dram_tensor("wmod", [DM, 6 * DM], F32,
                            kind="ExternalInput").ap()
    out_d = nc.dram_tensor("out", [OWN, DM], F32, kind="ExternalOutput").ap()

    def scr(name, shape):
        kind = "ExternalOutput" if name in taps else "Internal"
        return nc.dram_tensor(name, shape, F32, kind=kind).ap()

    ks_d = scr("Ks", [KVH * HD, S])
    vs_d = scr("Vs", [S, KVH * HD])
    qs_d = scr("Qs", [DM, OWN])
    x2_d = scr("X2s", [OWN, DM])
    mod_d = scr("MODs", [1, 6 * DM])
    acc_d = scr("ACCs", [OWN, 16 * 130]) if "ACCs" in taps else None
    hs_d = scr("Hs", [S, DM]) if "Hs" in taps else None
    h2t_d = scr("H2s", [DM, OWN]) if "H2s" in taps else None
    lg_d = scr("LGs", [OWN, 16]) if "LGs" in taps else None
    cmb_d = scr("CMBs", [OWN, 16]) if "CMBs" in taps else None

    pb = [nc.alloc_psum_tensor(f"pb{i}", [128, 512], F32) for i in range(8)]

    A = Arena(nc)
    ident = A.al(128, 128)
    ones_r = A.al(1, 128)
    ones_c = A.al(128, 1)
    c_fm = A.al(128, 16)
    s_fm = A.al(128, 16)
    kp_adj = A.al(128, 64)
    qpa_bc = A.al(128, OWN)
    rwT = A.al(128, 256)
    rb_bc = A.al(128, 16)
    vsum = A.al(1, KVH * HD)
    vsum_bc = A.al(128, KVH * HD)
    freqc = A.al(64, 1)
    flo = A.al(64, 1)
    bh_col = A.al(128, 16)
    epsc = A.al(128, 1)
    combine = A.al(128, 16 * NQT)

    sem_handles = []
    for nm in ("s_x", "s_w", "s_sp", "s_mm", "s_ve", "s_ac", "s_gp"):
        cm = nc.semaphore(nm)
        sem_handles.append(cm.__enter__())
    s_x, s_w, s_sp, s_mm, s_ve, s_ac, s_gp = [Ctr(h) for h in sem_handles]


    def _finish():
        with nc.Block() as blk:
            @blk.gpsimd
            def _(gp):
                gp.wait_ge(s_sp.h, s_sp.n)
                gp.wait_ge(s_x.h, s_x.n)
                gp.wait_ge(s_w.h, s_w.n)
        return nc
    A.push()
    tokr = A.al(1, S)

    A.push()
    mod6 = A.al(1, 6 * DM)
    mrow = A.al(1, DM)
    wm_sb = [A.al(128, 4096) for _ in range(2)]
    rw_sb = A.al(16, DM)
    rb_sb = A.al(1, 16)
    kp_raw = A.al(128, 64)

    with nc.Block() as blk:
        @blk.sync
        def _(sync):
            s_x.dma(sync.dma_start(out=tokr[:, :], in_=tokr_d[:, :]))
            s_x.dma(sync.dma_start(out=kp_raw[:, :], in_=tokc_d[:, :]))
            s_x.dma(sync.dma_start(
                out=c_fm[:, :],
                in_=c_d.rearrange("o (k p) -> (o p) k", p=128),
                allow_slow_non_contiguous=True))
            s_x.dma(sync.dma_start(out=rw_sb[:, :], in_=rw_d[:, :]))
            s_x.dma(sync.dma_start(out=rb_sb[:, :], in_=rb_d[:, :]))
        ld0 = s_x.n

        @blk.gpsimd
        def _(gp):
            gp.iota(ident[:, :], [[1, 128]], base=0, channel_multiplier=-1,
                    allow_small_or_imprecise_dtypes=True)
            gp.memset(ones_r[:, :], 1.0)
            gp.memset(ones_c[:, :], 1.0)
            gp.memset(epsc[:, :], 1e-5)
            for h in range(16):
                gp.memset(bh_col[:, h:h + 1], -B_H[h])
            for i in range(64):
                gp.memset(mrow[0:1, 1024 + i:1025 + i], FRQ_SP[i][0])
                gp.memset(mrow[0:1, 1088 + i:1089 + i], FRQ_SP[i][1])
            s_gp.inc(gp.engine_nop())
        gp0 = s_gp.n

        @blk.vector
        def _(ve):
            ve.wait_ge(s_gp.h, gp0)
            ve.tensor_single_scalar(ident[:, :], ident[:, :], 0.0,
                                    ALU.is_equal)
            ve.wait_ge(s_x.h, ld0)
            ve.tensor_single_scalar(kp_adj[:, :], kp_raw[:, :], 64.0,
                                    ALU.is_lt)
            ve.drain()
            ve.scalar_tensor_tensor(kp_adj[:, :], kp_adj[:, :], -2.0 * HUGE,
                                    kp_raw[:, :], ALU.mult, ALU.add)
            ve.tensor_single_scalar(mrow[:, 0:OWN], tokr[:, 0:OWN], 64.0,
                                    ALU.is_lt)
            ve.drain()
            ve.scalar_tensor_tensor(mrow[:, 0:OWN], mrow[:, 0:OWN], HUGE,
                                    tokr[:, 0:OWN], ALU.mult, ALU.add)
            s_ve.inc(ve.engine_nop())
        ve0 = s_ve.n

        @blk.scalar
        def _(sc):
            sc.wait_ge(s_ve.h, ve0)
            s_ac.inc(sc.activation(s_fm[:, :], c_fm[:, :], AF.Silu))

        @blk.tensor
        def _(te):
            te.wait_ge(s_ve.h, ve0)
            te.matmul(pb[7][0:64, 0:1], mrow[0:1, 1024:1088],
                      ident[0:1, 0:1], is_transpose=True,
                      start=True, stop=True)
            te.matmul(pb[7][0:64, 1:2], mrow[0:1, 1088:1152],
                      ident[0:1, 0:1], is_transpose=True,
                      start=True, stop=True)
            for i in range(2):
                te.matmul(pb[i][:, :], ones_r[:, :],
                          mrow[:, i * 512:(i + 1) * 512],
                          start=True, stop=True)
            m = te.matmul(pb[2][:, 0:16], ones_r[:, :], rb_sb[:, :],
                          start=True, stop=True)
            s_mm.inc(m)
        mm0 = s_mm.n

        @blk.vector
        def _(ve):
            ve.wait_ge(s_mm.h, mm0)
            ve.tensor_copy(freqc[:, :], pb[7][0:64, 0:1])
            ve.tensor_copy(flo[:, :], pb[7][0:64, 1:2])
            for i in range(2):
                ve.tensor_copy(qpa_bc[:, i * 512:(i + 1) * 512], pb[i][:, :])
            ve.tensor_copy(rb_bc[:, :], pb[2][:, 0:16])
            s_ve.inc(ve.engine_nop())

        for g in range(4):
            @blk.tensor
            def _(te, g=g):
                te.wait_ge(s_ve.h, s_ve.n)
                m = None
                for j in range(4):
                    dc = g * 4 + j
                    m = te.matmul(pb[3 + j][:, 0:16],
                                  rw_sb[:, dc * 128:(dc + 1) * 128],
                                  ident[0:16, 0:16], is_transpose=True,
                                  start=True, stop=True)
                s_mm.inc(m)
            gmk = s_mm.n

            @blk.vector
            def _(ve, g=g, gmk=gmk):
                ve.wait_ge(s_mm.h, gmk)
                for j in range(4):
                    dc = g * 4 + j
                    ve.tensor_copy(rwT[:, dc * 16:(dc + 1) * 16],
                                   pb[3 + j][:, 0:16])
                s_ve.inc(ve.engine_nop())

    with nc.Block() as blk:
        mm_marks = []
        for g in range(3):
            for dc in range(16):
                @blk.scalar
                def _(sc, g=g, dc=dc):
                    idx = g * 16 + dc
                    if idx >= 2:
                        sc.wait_ge(s_mm.h, mm_marks[idx - 2])
                    s_w.dma(sc.dma_start(
                        out=wm_sb[idx % 2][:, :],
                        in_=wmod_d[dc * 128:(dc + 1) * 128,
                                   g * 4096:(g + 1) * 4096]))
                dmark = s_w.n

                @blk.tensor
                def _(te, g=g, dc=dc, dmark=dmark):
                    te.wait_ge(s_w.h, dmark)
                    if dc == 0:
                        te.wait_ge(s_ve.h, s_ve.n)
                    m = None
                    for jt in range(8):
                        m = te.matmul(
                            pb[jt][0:1, :], s_fm[:, dc:dc + 1],
                            wm_sb[(g * 16 + dc) % 2][:,
                                                     jt * 512:(jt + 1) * 512],
                            start=(dc == 0), stop=(dc == 15))
                    s_mm.inc(m)
                mm_marks.append(s_mm.n)

            @blk.vector
            def _(ve, g=g):
                ve.wait_ge(s_mm.h, mm_marks[-1])
                for jt in range(8):
                    ve.tensor_copy(
                        mod6[:, g * 4096 + jt * 512:g * 4096 + (jt + 1) * 512],
                        pb[jt][0:1, :])
                s_ve.inc(ve.engine_nop())

        @blk.vector
        def _(ve):
            for k in (1, 4):
                sl = slice(k * DM, (k + 1) * DM)
                ve.tensor_single_scalar(mod6[:, sl], mod6[:, sl], 1.0,
                                        ALU.add)
            for k in (2, 5):
                sl = slice(k * DM, (k + 1) * DM)
                ve.tensor_single_scalar(mrow[:, :], mod6[:, sl], 0.0,
                                        ALU.is_ge)
                ve.drain()
                ve.tensor_single_scalar(mrow[:, :], mrow[:, :], 2.0,
                                        ALU.mult)
                ve.drain()
                ve.tensor_single_scalar(mrow[:, :], mrow[:, :], -1.0,
                                        ALU.add)
                ve.drain()
                ve.tensor_tensor(mod6[:, sl], mod6[:, sl], mrow[:, :],
                                 ALU.mult)
                ve.drain()
                ve.tensor_single_scalar(mod6[:, sl], mod6[:, sl], 0.25,
                                        ALU.max)
                ve.drain()
                ve.tensor_tensor(mod6[:, sl], mod6[:, sl], mrow[:, :],
                                 ALU.mult)
                ve.drain()
            s_ve.inc(ve.engine_nop())
        vem = s_ve.n

        @blk.gpsimd
        def _(gp):
            gp.wait_ge(s_ve.h, vem)
            s_sp.dma(gp.dma_start(out=mod_d[:, :], in_=mod6[:, :]))
            gp.wait_ge(s_sp.h, s_sp.n)

    A.pop()
    if stop_after == 'P0':
        return _finish()

    A.push()
    ca1_bc = A.al(128, DM)
    sa_bc = A.al(128, DM)

    A.push()
    mrow2 = A.al(1, 2 * DM)
    with nc.Block() as blk:
        @blk.sync
        def _(sync):
            s_x.dma(sync.dma_start(out=mrow2[:, 0:DM],
                                   in_=mod_d[:, DM:2 * DM]))
            s_x.dma(sync.dma_start(out=mrow2[:, DM:2 * DM],
                                   in_=mod_d[:, 0:DM]))
        ldm = s_x.n

        @blk.tensor
        def _(te):
            te.wait_ge(s_x.h, ldm)
            m = None
            for i in range(4):
                te.matmul(pb[i][:, :], ones_r[:, :],
                          mrow2[:, i * 512:(i + 1) * 512],
                          start=True, stop=True)
                m = te.matmul(pb[4 + i][:, :], ones_r[:, :],
                              mrow2[:, DM + i * 512:DM + (i + 1) * 512],
                              start=True, stop=True)
            s_mm.inc(m)
        mmb = s_mm.n

        @blk.vector
        def _(ve):
            ve.wait_ge(s_mm.h, mmb)
            for i in range(4):
                ve.tensor_copy(ca1_bc[:, i * 512:(i + 1) * 512], pb[i][:, :])
                ve.tensor_copy(sa_bc[:, i * 512:(i + 1) * 512],
                               pb[4 + i][:, :])
            s_ve.inc(ve.engine_nop())
    A.pop()

    for w in range(p1_nw):
        own = w < 2
        A.push()
        x_sb = A.al(128, DM)
        h_tm = A.al(128, DM)
        h_fm = [A.al(128, TW) for _ in range(16)]
        sct = A.al(128, 8)
        posw = A.al(64, TW)
        t_ang = A.al(64, TW)
        t_sin = A.al(64, TW)
        t_cos = A.al(64, TW)
        t_tmp = A.al(64, TW)
        t_nf = A.al(64, TW)
        t_sq = A.al(64, TW)
        t_cq = A.al(64, TW)
        tabi = A.al(64, TW, I32)
        rsc1 = A.al(64, TW)
        rsc2 = A.al(64, TW)
        wstr = [A.al(128, 1024) for _ in range(2)]
        ksb = A.al(128, 8 * TW)
        vsb = [A.al(128, 1024) for _ in range(4)]
        qsb = A.al(128, 8 * TW) if own else None
        cxbc = A.al(128, TW) if own else None
        cxrow = A.al(1, TW) if own else None

        with nc.Block() as blk:
            @blk.tensor
            def _(te):
                m = te.matmul(pb[0][0:64, :], ones_r[:, 0:64],
                              tokr[:, w * TW:(w + 1) * TW],
                              start=True, stop=True)
                s_mm.inc(m)
            posmk = s_mm.n

            @blk.vector
            def _(ve):
                ve.wait_ge(s_mm.h, posmk)
                ve.wait_ge(s_sp.h, s_sp.n)
                ve.tensor_copy(posw[:, :], pb[0][0:64, :])
                ve.tensor_scalar(t_ang[:, :], posw[:, :], freqc[:, 0:1],
                                 None, ALU.mult)
                ve.tensor_single_scalar(t_tmp[:, :], t_ang[:, :], INV2PI,
                                        ALU.mult)
                ve.tensor_copy(tabi[:, :], t_tmp[:, :])
                ve.tensor_copy(t_nf[:, :], tabi[:, :])
                ve.scalar_tensor_tensor(t_tmp[:, :], t_nf[:, :], -CW1,
                                        t_ang[:, :], ALU.mult, ALU.add)
                ve.scalar_tensor_tensor(t_tmp[:, :], t_nf[:, :], -CW2,
                                        t_tmp[:, :], ALU.mult, ALU.add)
                ve.scalar_tensor_tensor(t_sin[:, :], t_nf[:, :], -CW3,
                                        t_tmp[:, :], ALU.mult, ALU.add)
                ve.drain()
                ve.scalar_tensor_tensor(t_sin[:, :], posw[:, :], flo[:, 0:1],
                                        t_sin[:, :], ALU.mult, ALU.add)
                ve.tensor_single_scalar(t_tmp[:, :], t_ang[:, :], INV2PI,
                                        ALU.mult)
                ve.tensor_single_scalar(t_tmp[:, :], t_tmp[:, :], 0.25,
                                        ALU.add)
                ve.tensor_copy(tabi[:, :], t_tmp[:, :])
                ve.tensor_copy(t_nf[:, :], tabi[:, :])
                ve.scalar_tensor_tensor(t_tmp[:, :], t_nf[:, :], -CW1,
                                        t_ang[:, :], ALU.mult, ALU.add)
                ve.scalar_tensor_tensor(t_tmp[:, :], t_nf[:, :], -CW2,
                                        t_tmp[:, :], ALU.mult, ALU.add)
                ve.scalar_tensor_tensor(t_tmp[:, :], t_nf[:, :], -CW3,
                                        t_tmp[:, :], ALU.mult, ALU.add)
                ve.tensor_single_scalar(t_tmp[:, :], t_tmp[:, :], HPI_HI,
                                        ALU.add)
                ve.drain()
                ve.tensor_single_scalar(t_cos[:, :], t_tmp[:, :], HPI_LO,
                                        ALU.add)
                ve.drain()
                ve.scalar_tensor_tensor(t_cos[:, :], posw[:, :], flo[:, 0:1],
                                        t_cos[:, :], ALU.mult, ALU.add)
                s_ve.inc(ve.engine_nop())
            tabmk = s_ve.n

            @blk.scalar
            def _(sc):
                sc.wait_ge(s_ve.h, tabmk)
                sc.activation(t_sin[:, :], t_sin[:, :], AF.Sin)
                s_ac.inc(sc.activation(t_cos[:, :], t_cos[:, :], AF.Sin))
            tabac = s_ac.n

            if own:
                @blk.vector
                def _(ve):
                    ve.wait_ge(s_ac.h, tabac)
                    ve.tensor_single_scalar(t_sq[:, :], t_sin[:, :], SC_HI,
                                            ALU.mult)
                    ve.scalar_tensor_tensor(t_sq[:, :], t_sin[:, :], SC_LO,
                                            t_sq[:, :], ALU.mult, ALU.add)
                    ve.tensor_single_scalar(t_cq[:, :], t_cos[:, :], SC_HI,
                                            ALU.mult)
                    ve.scalar_tensor_tensor(t_cq[:, :], t_cos[:, :], SC_LO,
                                            t_cq[:, :], ALU.mult, ALU.add)
                    s_ve.inc(ve.engine_nop())

            for tt in range(4):
                row0 = w * TW + tt * 128

                if 'ln' in p1_parts:
                    @blk.sync
                    def _(sync, row0=row0):
                        sync.wait_ge(s_mm.h, s_mm.n)
                        s_x.dma(sync.dma_start(out=x_sb[:, :],
                                               in_=x_d[row0:row0 + 128, :]))
                    xmk = s_x.n

                    @blk.scalar
                    def _(sc, xmk=xmk):
                        sc.wait_ge(s_x.h, xmk)
                        sc.wait_ge(s_ve.h, s_ve.n)
                        s_ac.inc(sc.activation(h_tm[:, :], x_sb[:, :], AF.Square,
                                               accum_out=sct[:, 1:2]))
                    sqmk = s_ac.n

                    @blk.vector
                    def _(ve, xmk=xmk, sqmk=sqmk):
                        ve.wait_ge(s_x.h, xmk)
                        ve.reduce_sum(sct[:, 0:1], x_sb[:, :], AX.X)
                        ve.wait_ge(s_ac.h, sqmk)
                        ve.tensor_single_scalar(sct[:, 2:3], sct[:, 0:1],
                                                1.0 / DM, ALU.mult)
                        ve.drain()
                        ve.tensor_tensor(sct[:, 3:4], sct[:, 2:3], sct[:, 2:3],
                                         ALU.mult)
                        ve.tensor_single_scalar(sct[:, 4:5], sct[:, 1:2],
                                                1.0 / DM, ALU.mult)
                        ve.drain()
                        ve.tensor_tensor(sct[:, 4:5], sct[:, 4:5], sct[:, 3:4],
                                         ALU.subtract)
                        ve.drain()
                        s_ve.inc(ve.engine_nop())
                    stmk = s_ve.n

                    @blk.scalar
                    def _(sc, stmk=stmk):
                        sc.wait_ge(s_ve.h, stmk)
                        s_ac.inc(sc.activation(sct[:, 5:6], sct[:, 4:5], AF.Sqrt,
                                               bias=epsc[:, 0:1]))
                    sdmk = s_ac.n

                    @blk.vector
                    def _(ve, sdmk=sdmk, own=own):
                        ve.wait_ge(s_ac.h, sdmk)
                        ve.reciprocal(sct[:, 6:7], sct[:, 5:6])
                        ve.drain()
                        ve.tensor_scalar(h_tm[:, :], x_sb[:, :], sct[:, 2:3],
                                         sct[:, 6:7], ALU.subtract, ALU.mult)
                        ve.tensor_tensor(h_tm[:, :], h_tm[:, :], ca1_bc[:, :],
                                         ALU.mult)
                        ve.tensor_tensor(h_tm[:, :], h_tm[:, :], sa_bc[:, :],
                                         ALU.add)
                        if own:
                            ve.reduce_sum(sct[:, 7:8], h_tm[:, :], AX.X,
                                          apply_absolute_value=True)
                        s_ve.inc(ve.engine_nop())
                    hmk = s_ve.n
                    if hs_d is not None:
                        @blk.gpsimd
                        def _(gp, row0=row0, hmk=hmk):
                            gp.wait_ge(s_ve.h, hmk)
                            s_sp.dma(gp.dma_start(
                                out=hs_d[row0:row0 + 128, :],
                                in_=h_tm[:, :]))
                            gp.wait_ge(s_sp.h, s_sp.n)
                else:
                    hmk = s_ve.n

                if own and 'cx' in p1_parts:
                    @blk.scalar
                    def _(sc, hmk=hmk):
                        sc.wait_ge(s_ve.h, hmk)
                        s_ac.inc(sc.activation(sct[:, 7:8], sct[:, 7:8],
                                               AF.Sigmoid))
                    cxmk = s_ac.n

                    @blk.tensor
                    def _(te, cxmk=cxmk):
                        te.wait_ge(s_ac.h, cxmk)
                        m = te.matmul(pb[0][0:1, 0:128], sct[:, 7:8],
                                      ident[:, :], is_transpose=True,
                                      start=True, stop=True)
                        s_mm.inc(m)
                    cxt = s_mm.n

                    @blk.vector
                    def _(ve, tt=tt, cxt=cxt):
                        ve.wait_ge(s_mm.h, cxt)
                        ve.tensor_copy(cxrow[:, tt * 128:(tt + 1) * 128],
                                       pb[0][0:1, 0:128])
                        s_ve.inc(ve.engine_nop())

                if 'tr' in p1_parts:
                    for half in range(2):
                        @blk.tensor
                        def _(te, half=half):
                            te.wait_ge(s_ve.h, s_ve.n)
                            m = None
                            for j in range(8):
                                dc = half * 8 + j
                                m = te.matmul(pb[j][:, 0:128],
                                              h_tm[:, dc * 128:(dc + 1) * 128],
                                              ident[:, :], is_transpose=True,
                                              start=True, stop=True)
                            s_mm.inc(m)
                        tmk = s_mm.n

                        @blk.vector
                        def _(ve, half=half, tt=tt, tmk=tmk):
                            ve.wait_ge(s_mm.h, tmk)
                            for j in range(8):
                                dc = half * 8 + j
                                ve.tensor_copy(
                                    h_fm[dc][:, tt * 128:(tt + 1) * 128],
                                    pb[j][:, 0:128])
                            s_ve.inc(ve.engine_nop())

            tr_done = s_ve.n
            if own:
                @blk.tensor
                def _(te):
                    te.wait_ge(s_ve.h, tr_done)
                    m = te.matmul(pb[0][:, :], ones_r[:, :], cxrow[:, :],
                                  start=True, stop=True)
                    s_mm.inc(m)
                cbt = s_mm.n

                @blk.vector
                def _(ve, cbt=cbt):
                    ve.wait_ge(s_mm.h, cbt)
                    ve.tensor_copy(cxbc[:, :], pb[0][:, :])
                    s_ve.inc(ve.engine_nop())
                tr_done = s_ve.n

            if 'K' in p1_parts:
                kmarks = []
                for dc in range(16):
                    @blk.scalar
                    def _(sc, dc=dc):
                        if dc >= 2:
                            sc.wait_ge(s_mm.h, kmarks[dc - 2])
                        s_w.dma(sc.dma_start(
                            out=wstr[dc % 2][:, :],
                            in_=wkv_d[dc * 128:(dc + 1) * 128, 0:1024]))
                    wmk = s_w.n

                    @blk.tensor
                    def _(te, dc=dc, wmk=wmk):
                        te.wait_ge(s_w.h, wmk)
                        if dc == 0:
                            te.wait_ge(s_ve.h, tr_done)
                        m = None
                        for kf in range(8):
                            m = te.matmul(pb[kf][:, :],
                                          wstr[dc % 2][:, kf * 128:(kf + 1) * 128],
                                          h_fm[dc][:, :],
                                          start=(dc == 0), stop=(dc == 15))
                        s_mm.inc(m)
                    kmarks.append(s_mm.n)

                @blk.vector
                def _(ve):
                    ve.wait_ge(s_mm.h, kmarks[-1])
                    ve.wait_ge(s_ac.h, tabac)
                    for kf in range(8):
                        dst = ksb[:, kf * TW:(kf + 1) * TW]
                        ve.tensor_tensor(rsc1[:, :], pb[kf][0:64, :], t_cos[:, :],
                                         ALU.mult)
                        ve.tensor_tensor(rsc2[:, :], pb[kf][64:128, :],
                                         t_sin[:, :], ALU.mult)
                        ve.tensor_tensor(dst[0:64, :], rsc1[:, :], rsc2[:, :],
                                         ALU.subtract)
                        ve.tensor_tensor(rsc1[:, :], pb[kf][0:64, :], t_sin[:, :],
                                         ALU.mult)
                        ve.tensor_tensor(rsc2[:, :], pb[kf][64:128, :],
                                         t_cos[:, :], ALU.mult)
                        ve.tensor_tensor(dst[64:128, :], rsc1[:, :], rsc2[:, :],
                                         ALU.add)
                    s_ve.inc(ve.engine_nop())
                kspmk = s_ve.n

                @blk.gpsimd
                def _(gp):
                    gp.wait_ge(s_ve.h, kspmk)
                    for kf in range(8):
                        s_sp.dma(gp.dma_start(
                            out=ks_d[kf * 128:(kf + 1) * 128, w * TW:(w + 1) * TW],
                            in_=ksb[:, kf * TW:(kf + 1) * TW]))

            else:
                kspmk = s_ve.n
            if 'V' in p1_parts:
                vmarks = []
                for dc in range(16):
                    @blk.scalar
                    def _(sc, dc=dc):
                        if dc >= 2:
                            sc.wait_ge(s_mm.h, vmarks[dc - 2])
                        else:
                            sc.wait_ge(s_mm.h, kmarks[-1])
                        s_w.dma(sc.dma_start(
                            out=wstr[dc % 2][:, :],
                            in_=wkv_d[dc * 128:(dc + 1) * 128, 1024:2048]))
                    wmk = s_w.n

                    @blk.tensor
                    def _(te, dc=dc, wmk=wmk):
                        te.wait_ge(s_w.h, wmk)
                        if dc == 0:
                            te.wait_ge(s_ve.h, kspmk)
                        m = None
                        for tt in range(4):
                            for fh in range(2):
                                m = te.matmul(
                                    pb[tt * 2 + fh][:, :],
                                    h_fm[dc][:, tt * 128:(tt + 1) * 128],
                                    wstr[dc % 2][:, fh * 512:(fh + 1) * 512],
                                    start=(dc == 0), stop=(dc == 15))
                        s_mm.inc(m)
                    vmarks.append(s_mm.n)

                @blk.vector
                def _(ve):
                    ve.wait_ge(s_mm.h, vmarks[-1])
                    for tt in range(4):
                        for fh in range(2):
                            ve.tensor_copy(vsb[tt][:, fh * 512:(fh + 1) * 512],
                                           pb[tt * 2 + fh][:, :])
                    s_ve.inc(ve.engine_nop())
                vspmk = s_ve.n

                @blk.gpsimd
                def _(gp):
                    gp.wait_ge(s_ve.h, vspmk)
                    for tt in range(4):
                        r0 = w * TW + tt * 128
                        s_sp.dma(gp.dma_start(out=vs_d[r0:r0 + 128, :],
                                              in_=vsb[tt][:, :]))

            if 'Q' in p1_parts:
                if own:
                    prev_q = vmarks[-1]
                    for rnd in range(2):
                        qmarks = []
                        for dc in range(16):
                            @blk.scalar
                            def _(sc, rnd=rnd, dc=dc, prev_q=prev_q):
                                if dc >= 2:
                                    sc.wait_ge(s_mm.h, qmarks[dc - 2])
                                else:
                                    sc.wait_ge(s_mm.h, prev_q)
                                s_w.dma(sc.dma_start(
                                    out=wstr[dc % 2][:, :],
                                    in_=wq_d[dc * 128:(dc + 1) * 128,
                                             rnd * 1024:(rnd + 1) * 1024]))
                            wmk = s_w.n

                            @blk.tensor
                            def _(te, dc=dc, wmk=wmk):
                                te.wait_ge(s_w.h, wmk)
                                if dc == 0:
                                    te.wait_ge(s_ve.h, s_ve.n)
                                m = None
                                for qf in range(8):
                                    m = te.matmul(
                                        pb[qf][:, :],
                                        wstr[dc % 2][:, qf * 128:(qf + 1) * 128],
                                        h_fm[dc][:, :],
                                        start=(dc == 0), stop=(dc == 15))
                                s_mm.inc(m)
                            qmarks.append(s_mm.n)

                        @blk.vector
                        def _(ve, qmarks=qmarks):
                            ve.wait_ge(s_mm.h, qmarks[-1])
                            for qf in range(8):
                                dst = qsb[:, qf * TW:(qf + 1) * TW]
                                ve.tensor_tensor(rsc1[:, :], pb[qf][0:64, :],
                                                 t_cq[:, :], ALU.mult)
                                ve.tensor_tensor(rsc2[:, :], pb[qf][64:128, :],
                                                 t_sq[:, :], ALU.mult)
                                ve.tensor_tensor(dst[0:64, :], rsc1[:, :],
                                                 rsc2[:, :], ALU.subtract)
                                ve.tensor_tensor(rsc1[:, :], pb[qf][0:64, :],
                                                 t_sq[:, :], ALU.mult)
                                ve.tensor_tensor(rsc2[:, :], pb[qf][64:128, :],
                                                 t_cq[:, :], ALU.mult)
                                ve.tensor_tensor(dst[64:128, :], rsc1[:, :],
                                                 rsc2[:, :], ALU.add)
                                ve.tensor_tensor(dst[0:64, :], dst[0:64, :],
                                                 cxbc[0:64, :], ALU.mult)
                                ve.tensor_tensor(dst[64:128, :], dst[64:128, :],
                                                 cxbc[64:128, :], ALU.mult)
                            s_ve.inc(ve.engine_nop())
                        qspmk = s_ve.n

                        prev_q = qmarks[-1]

                        @blk.gpsimd
                        def _(gp, rnd=rnd, qspmk=qspmk):
                            gp.wait_ge(s_ve.h, qspmk)
                            for qf in range(8):
                                gqf = rnd * 8 + qf
                                s_sp.dma(gp.dma_start(
                                    out=qs_d[gqf * 128:(gqf + 1) * 128,
                                             w * TW:(w + 1) * TW],
                                    in_=qsb[:, qf * TW:(qf + 1) * TW]))
        A.pop()

    A.pop()
    A.pop()
    if stop_after == 'P1':
        return _finish()

    A.push()
    attnacc = [A.al(128, 16 * 130) for _ in range(NQT)]

    A.push()
    q_at = [A.al(128, OWN) for _ in range(HPP)]
    k_at = [A.al(128, CH) for _ in range(KVPP)]
    v_at = [A.al(128, KVPP * 130) for _ in range(8)]
    dist = [A.al(128, 512) for _ in range(16)]
    pt_sb = [A.al(128, 512) for _ in range(8)]
    ssc = [A.al(128, 512) for _ in range(2)]
    dsc = A.al(128, 512)

    with nc.Block() as blk:
        @blk.gpsimd
        def _(gp):
            for q in range(NQT):
                gp.memset(attnacc[q][:, :], 0.0)
            gp.memset(vsum[:, :], 0.0)
            gp.wait_ge(s_sp.h, s_sp.n)
            s_gp.inc(gp.engine_nop())

    for hp in range(NPASS):
        with nc.Block() as blk:
            @blk.sync
            def _(sync):
                for hl in range(HPP):
                    gh = hp * HPP + hl
                    s_x.dma(sync.dma_start(
                        out=q_at[hl][:, :],
                        in_=qs_d[gh * 128:(gh + 1) * 128, :]))

        for chk in range(NCH):
            with nc.Block() as blk:
                @blk.sync
                def _(sync):
                    for kvl in range(KVPP):
                        gkv = hp * KVPP + kvl
                        s_x.dma(sync.dma_start(
                            out=k_at[kvl][:, :],
                            in_=ks_d[gkv * 128:(gkv + 1) * 128,
                                     chk * CH:(chk + 1) * CH]))
                    for kt in range(8):
                        r0 = chk * CH + kt * 128
                        dst = v_at[kt].rearrange(
                            "k (b f) -> k b f", f=130)[:, :, 0:128]
                        src = vs_d[r0:r0 + 128,
                                   hp * 256:(hp + 1) * 256].rearrange(
                            "k (b f) -> k b f", b=KVPP)
                        s_x.dma(sync.dma_start(out=dst, in_=src))
                ldmk = s_x.n

                @blk.gpsimd
                def _(gp):
                    gp.wait_ge(s_x.h, ldmk)
                    for kt in range(8):
                        ones_ap = v_at[kt].rearrange(
                            "k (b f) -> k b f", f=130)[:, :, 128:129]
                        gp.memset(ones_ap, 1.0)
                    s_gp.inc(gp.engine_nop())
                vone = s_gp.n

                @blk.tensor
                def _(te):
                    te.wait_ge(s_x.h, ldmk)
                    te.wait_ge(s_gp.h, vone)
                    m = None
                    for kvl in range(KVPP):
                        for kt in range(8):
                            m = te.matmul(
                                pb[7][0:1, kvl * 128:(kvl + 1) * 128],
                                ones_c[:, :],
                                v_at[kt][:, kvl * 130:kvl * 130 + 128],
                                start=(kt == 0), stop=(kt == 7))
                    s_mm.inc(m)
                vsmk = s_mm.n

                @blk.vector
                def _(ve, hp=hp, vsmk=vsmk):
                    ve.wait_ge(s_mm.h, vsmk)
                    sl = slice(hp * 256, (hp + 1) * 256)
                    ve.tensor_tensor(vsum[:, sl], vsum[:, sl],
                                     pb[7][0:1, 0:256], ALU.add)
                    s_ve.inc(ve.engine_nop())

                bank_free = [s_ve.n, s_ve.n]
                ssc_free = [s_ac.n, s_ac.n]
                pv_bank_free = [s_ve.n] * 4
                pt_free = s_mm.n
                for qt in range(2):
                    @blk.vector
                    def _(ve, qt=qt, chk=chk):
                        for kt in range(8):
                            d = dist[qt * 8 + kt]
                            ve.tensor_single_scalar(
                                d[:, :], qpa_bc[:, qt * 512:(qt + 1) * 512],
                                kp_adj[:, chk * 8 + kt:chk * 8 + kt + 1],
                                ALU.subtract)
                            ve.tensor_single_scalar(dsc[:, :], d[:, :],
                                                    -1.0, ALU.mult)
                            ve.tensor_tensor(d[:, :], d[:, :], dsc[:, :],
                                             ALU.max)
                            ve.tensor_single_scalar(dsc[:, :], d[:, :],
                                                    256.5, ALU.is_le)
                            ve.scalar_tensor_tensor(d[:, :], dsc[:, :],
                                                    HUGE, d[:, :],
                                                    ALU.mult, ALU.add)
                        s_ve.inc(ve.engine_nop())

                    for hl in range(HPP):
                        gh = hp * HPP + hl
                        kvl = hl // 2
                        shi, slo = SLOPE_SP[gh]
                        mm0, ve0, ac0 = s_mm.n, s_ve.n, s_ac.n
                        mm_mark = [mm0 + kt + 1 for kt in range(8)]
                        pv_mark = [mm0 + 9 + qq for qq in range(4)]
                        al_mark = [ve0 + kt + 1 for kt in range(8)]
                        ev_mark = [ve0 + 9 + qq for qq in range(4)]
                        ex_mark = [ac0 + kt + 1 for kt in range(8)]
                        bf_in = list(bank_free)
                        sf_in = list(ssc_free)
                        pvb_in = list(pv_bank_free)
                        ptf_in = pt_free

                        @blk.tensor
                        def _(te, hl=hl, kvl=kvl, qt=qt, bf_in=bf_in,
                              pvb_in=pvb_in, al_mark=al_mark,
                              ex_mark=ex_mark):
                            for kt in range(8):
                                te.wait_ge(s_ve.h,
                                           al_mark[kt - 2] if kt >= 2
                                           else bf_in[kt % 2])
                                m = te.matmul(
                                    pb[kt % 2][:, :],
                                    k_at[kvl][:, kt * 128:(kt + 1) * 128],
                                    q_at[hl][:, qt * 512:(qt + 1) * 512],
                                    start=True, stop=True)
                                s_mm.inc(m)
                            for qq in range(4):
                                if qq == 0:
                                    te.wait_ge(s_ac.h, ex_mark[7])
                                te.wait_ge(s_ve.h, pvb_in[qq])
                                m = None
                                for kt in range(8):
                                    m = te.matmul(
                                        pb[2 + qq][:, 0:129],
                                        pt_sb[kt][:, qq * 128:(qq + 1) * 128],
                                        v_at[kt][:, kvl * 130:kvl * 130 + 129],
                                        start=(kt == 0), stop=(kt == 7))
                                s_mm.inc(m)

                        @blk.vector
                        def _(ve, gh=gh, qt=qt, shi=shi, slo=slo,
                              mm_mark=mm_mark, pv_mark=pv_mark,
                              sf_in=sf_in):
                            for kt in range(8):
                                ve.wait_ge(s_mm.h, mm_mark[kt])
                                ve.wait_ge(s_ac.h,
                                           ex_mark[kt - 2] if kt >= 2
                                           else sf_in[kt % 2])
                                d = dist[qt * 8 + kt]
                                sb = ssc[kt % 2]
                                st = ve.scalar_tensor_tensor(
                                    sb[:, :], d[:, :], -shi,
                                    pb[kt % 2][:, :], ALU.mult, ALU.add)
                                if slo != 0.0:
                                    st = ve.scalar_tensor_tensor(
                                        sb[:, :], d[:, :], -slo, sb[:, :],
                                        ALU.mult, ALU.add)
                                s_ve.inc(st)
                            for qq in range(4):
                                ve.wait_ge(s_mm.h, pv_mark[qq])
                                qidx = qt * 4 + qq
                                a = attnacc[qidx][:, gh * 130:gh * 130 + 129]
                                s_ve.inc(ve.tensor_tensor(
                                    a, a, pb[2 + qq][:, 0:129], ALU.add))

                        @blk.scalar
                        def _(sc, gh=gh, al_mark=al_mark, ptf_in=ptf_in):
                            sc.wait_ge(s_mm.h, ptf_in)
                            for kt in range(8):
                                sc.wait_ge(s_ve.h, al_mark[kt])
                                s_ac.inc(sc.activation(
                                    pt_sb[kt][:, :], ssc[kt % 2][:, :],
                                    AF.Exp, bias=bh_col[:, gh:gh + 1]))

                        assert s_mm.n == mm0 + 12, (s_mm.n, mm0)
                        assert s_ve.n == ve0 + 12, (s_ve.n, ve0)
                        assert s_ac.n == ac0 + 8, (s_ac.n, ac0)
                        bank_free = [al_mark[6], al_mark[7]]
                        ssc_free = [ex_mark[6], ex_mark[7]]
                        pv_bank_free = list(ev_mark)
                        pt_free = pv_mark[3]

    with nc.Block() as blk:
        @blk.tensor
        def _(te):
            m = None
            for i in range(2):
                m = te.matmul(pb[i][:, :], ones_r[:, :],
                              vsum[:, i * 512:(i + 1) * 512],
                              start=True, stop=True)
            s_mm.inc(m)
        vbmk = s_mm.n

        @blk.vector
        def _(ve):
            ve.wait_ge(s_mm.h, vbmk)
            for i in range(2):
                ve.tensor_copy(vsum_bc[:, i * 512:(i + 1) * 512], pb[i][:, :])
            for q in range(NQT):
                for gh in range(16):
                    a = attnacc[q][:, gh * 130:gh * 130 + 128]
                    sm = attnacc[q][:, gh * 130 + 128:gh * 130 + 129]
                    u = ssc[0][:, 0:1]
                    den = ssc[0][:, 1:2]
                    ve.tensor_single_scalar(u, sm, 0.0, ALU.is_equal)
                    ve.drain()
                    ve.scalar_tensor_tensor(den, u, 8192.0, sm,
                                            ALU.mult, ALU.add)
                    ve.drain()
                    ve.reciprocal(den, den)
                    ve.drain()
                    vsl = vsum_bc[:, (gh // 2) * 128:(gh // 2) * 128 + 128]
                    ve.scalar_tensor_tensor(a, vsl, u, a, ALU.mult, ALU.add)
                    ve.tensor_single_scalar(a, a, den, ALU.mult)
            s_ve.inc(ve.engine_nop())
        nomk = s_ve.n

        if acc_d is not None:
            @blk.gpsimd
            def _(gp):
                gp.wait_ge(s_ve.h, nomk)
                for q in range(NQT):
                    s_sp.dma(gp.dma_start(
                        out=acc_d[q * 128:(q + 1) * 128, :],
                        in_=attnacc[q][:, :]))

    A.pop()
    if stop_after == 'P2':
        return _finish()

    A.push()
    bo_bc = A.al(128, DM)
    ga_bc = A.al(128, DM)
    wostr = [A.al(128, 512) for _ in range(2)]
    afm = [A.al(128, 128) for _ in range(2)]
    x2t = A.al(128, DM)
    xot = A.al(128, DM)
    mrow4 = A.al(1, DM)

    with nc.Block() as blk:
        @blk.sync
        def _(sync):
            s_x.dma(sync.dma_start(out=mrow4[:, :], in_=bo_d[:, :]))
        l1 = s_x.n

        @blk.tensor
        def _(te):
            te.wait_ge(s_x.h, l1)
            m = None
            for i in range(4):
                m = te.matmul(pb[i][:, :], ones_r[:, :],
                              mrow4[:, i * 512:(i + 1) * 512],
                              start=True, stop=True)
            s_mm.inc(m)
        m1 = s_mm.n

        @blk.vector
        def _(ve):
            ve.wait_ge(s_mm.h, m1)
            for i in range(4):
                ve.tensor_copy(bo_bc[:, i * 512:(i + 1) * 512], pb[i][:, :])
            s_ve.inc(ve.engine_nop())
        v1 = s_ve.n

        @blk.sync
        def _(sync):
            sync.wait_ge(s_ve.h, v1)
            s_x.dma(sync.dma_start(out=mrow4[:, :],
                                   in_=mod_d[:, 2 * DM:3 * DM]))
        l2 = s_x.n

        @blk.tensor
        def _(te):
            te.wait_ge(s_x.h, l2)
            m = None
            for i in range(4):
                m = te.matmul(pb[4 + i][:, :], ones_r[:, :],
                              mrow4[:, i * 512:(i + 1) * 512],
                              start=True, stop=True)
            s_mm.inc(m)
        m2 = s_mm.n

        @blk.vector
        def _(ve):
            ve.wait_ge(s_mm.h, m2)
            for i in range(4):
                ve.tensor_copy(ga_bc[:, i * 512:(i + 1) * 512],
                               pb[4 + i][:, :])
            s_ve.inc(ve.engine_nop())

    for q in range(NQT):
        with nc.Block() as blk:
            @blk.sync
            def _(sync, q=q):
                s_x.dma(sync.dma_start(out=xot[:, :],
                                       in_=x_d[q * 128:(q + 1) * 128, :]))
            xomk = s_x.n

            womk = []
            for fc in range(16):
                @blk.tensor
                def _(te, q=q, fc=fc):
                    te.wait_ge(s_ve.h, s_ve.n)
                    m = te.matmul(pb[4 + fc % 2][:, 0:128],
                                  attnacc[q][:, fc * 130:fc * 130 + 128],
                                  ident[:, :], is_transpose=True,
                                  start=True, stop=True)
                    s_mm.inc(m)
                tmk = s_mm.n

                @blk.vector
                def _(ve, fc=fc, tmk=tmk):
                    ve.wait_ge(s_mm.h, tmk)
                    ve.tensor_copy(afm[fc % 2][:, :], pb[4 + fc % 2][:, 0:128])
                    s_ve.inc(ve.engine_nop())
                emk = s_ve.n

                for gg in range(4):
                    @blk.scalar
                    def _(sc, fc=fc, gg=gg):
                        idx = fc * 4 + gg
                        if idx >= 2:
                            sc.wait_ge(s_mm.h, womk[idx - 2])
                        s_w.dma(sc.dma_start(
                            out=wostr[idx % 2][:, :],
                            in_=wo_d[fc * 128:(fc + 1) * 128,
                                     gg * 512:(gg + 1) * 512]))
                    wmk = s_w.n

                    @blk.tensor
                    def _(te, fc=fc, gg=gg, wmk=wmk, emk=emk):
                        te.wait_ge(s_w.h, wmk)
                        te.wait_ge(s_ve.h, emk)
                        m = te.matmul(pb[gg][:, :], afm[fc % 2][:, :],
                                      wostr[(fc * 4 + gg) % 2][:, :],
                                      start=(fc == 0), stop=(fc == 15))
                        s_mm.inc(m)
                    womk.append(s_mm.n)

            @blk.vector
            def _(ve, xomk=xomk):
                ve.wait_ge(s_mm.h, womk[-1])
                ve.wait_ge(s_x.h, xomk)
                for gg in range(4):
                    sl = slice(gg * 512, (gg + 1) * 512)
                    ve.tensor_tensor(x2t[:, sl], pb[gg][:, :], bo_bc[:, sl],
                                     ALU.add)
                    ve.tensor_tensor(x2t[:, sl], x2t[:, sl], ga_bc[:, sl],
                                     ALU.mult)
                    ve.tensor_tensor(x2t[:, sl], x2t[:, sl], xot[:, sl],
                                     ALU.add)
                s_ve.inc(ve.engine_nop())
            x2mk = s_ve.n

            @blk.gpsimd
            def _(gp, q=q, x2mk=x2mk):
                gp.wait_ge(s_ve.h, x2mk)
                s_sp.dma(gp.dma_start(out=x2_d[q * 128:(q + 1) * 128, :],
                                      in_=x2t[:, :]))
                gp.wait_ge(s_sp.h, s_sp.n)

    A.pop()
    A.pop()
    if stop_after == 'P4a':
        return _finish()

    A.push()
    h2_fm = [A.al(128, OWN) for _ in range(16)]
    A.push()
    cb1_bc = A.al(128, DM)
    sb_bc = A.al(128, DM)
    x2t = A.al(128, DM)
    h2t = A.al(128, DM)
    sct2 = A.al(128, 8)
    lgt = A.al(128, 64)
    mrow5 = A.al(1, 2 * DM)

    with nc.Block() as blk:
        @blk.sync
        def _(sync):
            s_x.dma(sync.dma_start(out=mrow5[:, 0:DM],
                                   in_=mod_d[:, 4 * DM:5 * DM]))
            s_x.dma(sync.dma_start(out=mrow5[:, DM:2 * DM],
                                   in_=mod_d[:, 3 * DM:4 * DM]))
        lm = s_x.n

        @blk.tensor
        def _(te):
            te.wait_ge(s_x.h, lm)
            m = None
            for i in range(4):
                te.matmul(pb[i][:, :], ones_r[:, :],
                          mrow5[:, i * 512:(i + 1) * 512],
                          start=True, stop=True)
                m = te.matmul(pb[4 + i][:, :], ones_r[:, :],
                              mrow5[:, DM + i * 512:DM + (i + 1) * 512],
                              start=True, stop=True)
            s_mm.inc(m)
        bm = s_mm.n

        @blk.vector
        def _(ve):
            ve.wait_ge(s_mm.h, bm)
            for i in range(4):
                ve.tensor_copy(cb1_bc[:, i * 512:(i + 1) * 512], pb[i][:, :])
                ve.tensor_copy(sb_bc[:, i * 512:(i + 1) * 512],
                               pb[4 + i][:, :])
            s_ve.inc(ve.engine_nop())

    for q in range(NQT):
        with nc.Block() as blk:
            @blk.sync
            def _(sync, q=q):
                s_x.dma(sync.dma_start(out=x2t[:, :],
                                       in_=x2_d[q * 128:(q + 1) * 128, :]))
            xmk = s_x.n

            @blk.vector
            def _(ve, xmk=xmk):
                ve.wait_ge(s_x.h, xmk)
                ve.reduce_sum(sct2[:, 0:1], x2t[:, :], AX.X)
                s_ve.inc(ve.engine_nop())
            r0mk = s_ve.n

            @blk.scalar
            def _(sc, xmk=xmk, r0mk=r0mk):
                sc.wait_ge(s_x.h, xmk)
                sc.wait_ge(s_ve.h, r0mk)
                s_ac.inc(sc.activation(h2t[:, :], x2t[:, :], AF.Square,
                                       accum_out=sct2[:, 1:2]))
            sqm = s_ac.n

            @blk.vector
            def _(ve, sqm=sqm):
                ve.wait_ge(s_ac.h, sqm)
                ve.tensor_single_scalar(sct2[:, 2:3], sct2[:, 0:1], 1.0 / DM,
                                        ALU.mult)
                ve.drain()
                ve.tensor_tensor(sct2[:, 3:4], sct2[:, 2:3], sct2[:, 2:3],
                                 ALU.mult)
                ve.tensor_single_scalar(sct2[:, 4:5], sct2[:, 1:2], 1.0 / DM,
                                        ALU.mult)
                ve.drain()
                ve.tensor_tensor(sct2[:, 4:5], sct2[:, 4:5], sct2[:, 3:4],
                                 ALU.subtract)
                ve.drain()
                s_ve.inc(ve.engine_nop())
            stm = s_ve.n

            @blk.scalar
            def _(sc, stm=stm):
                sc.wait_ge(s_ve.h, stm)
                s_ac.inc(sc.activation(sct2[:, 5:6], sct2[:, 4:5], AF.Sqrt,
                                       bias=epsc[:, 0:1]))
            sdm = s_ac.n

            @blk.vector
            def _(ve, sdm=sdm):
                ve.wait_ge(s_ac.h, sdm)
                ve.reciprocal(sct2[:, 6:7], sct2[:, 5:6])
                ve.drain()
                ve.tensor_scalar(h2t[:, :], x2t[:, :], sct2[:, 2:3],
                                 sct2[:, 6:7], ALU.subtract, ALU.mult)
                ve.tensor_tensor(h2t[:, :], h2t[:, :], cb1_bc[:, :], ALU.mult)
                ve.tensor_tensor(h2t[:, :], h2t[:, :], sb_bc[:, :], ALU.add)
                s_ve.inc(ve.engine_nop())
            h2mk = s_ve.n

            for half in range(2):
                @blk.tensor
                def _(te, half=half, h2mk=h2mk):
                    te.wait_ge(s_ve.h, h2mk if half == 0 else s_ve.n)
                    m = None
                    for j in range(8):
                        dc = half * 8 + j
                        m = te.matmul(pb[j][:, 0:128],
                                      h2t[:, dc * 128:(dc + 1) * 128],
                                      ident[:, :], is_transpose=True,
                                      start=True, stop=True)
                    s_mm.inc(m)
                tmk = s_mm.n

                @blk.vector
                def _(ve, half=half, q=q, tmk=tmk):
                    ve.wait_ge(s_mm.h, tmk)
                    for j in range(8):
                        dc = half * 8 + j
                        ve.tensor_copy(h2_fm[dc][:, q * 128:(q + 1) * 128],
                                       pb[j][:, 0:128])
                    s_ve.inc(ve.engine_nop())
            trh2 = s_ve.n

            @blk.tensor
            def _(te, q=q, trh2=trh2):
                te.wait_ge(s_ve.h, trh2)
                m = None
                for dc in range(16):
                    m = te.matmul(pb[6][:, 0:16],
                                  h2_fm[dc][:, q * 128:(q + 1) * 128],
                                  rwT[:, dc * 16:(dc + 1) * 16],
                                  start=(dc == 0), stop=(dc == 15))
                s_mm.inc(m)
            lgmk = s_mm.n

            @blk.vector
            def _(ve, lgmk=lgmk):
                ve.wait_ge(s_mm.h, lgmk)
                lg = lgt[:, 0:16]
                lb = lgt[:, 16:32]
                ws = lgt[:, 48:64]
                m1 = sct2[:, 0:1]
                ve.tensor_copy(lg, pb[6][:, 0:16])
                ve.drain()
                ve.tensor_tensor(lb, lg, rb_bc[:, :], ALU.add)
                ve.drain()
                ve.reduce_max(m1, lb, AX.X)
                ve.drain()
                ve.tensor_single_scalar(ws, lb, m1, ALU.is_equal)
                ve.drain()
                ve.scalar_tensor_tensor(lb, ws, -1e30, lb, ALU.mult, ALU.add)
                ve.drain()
                ve.reduce_max(m1, lb, AX.X)
                ve.drain()
                ve.tensor_single_scalar(lb, lb, m1, ALU.is_equal)
                ve.drain()
                ve.tensor_tensor(ws, ws, lb, ALU.add)
                ve.drain()
                ve.reduce_max(m1, lg, AX.X, negate=True)
                ve.drain()
                s_ve.inc(ve.engine_nop())
            r1mk = s_ve.n

            @blk.scalar
            def _(sc, r1mk=r1mk):
                sc.wait_ge(s_ve.h, r1mk)
                s_ac.inc(sc.activation(lgt[:, 32:48], lgt[:, 0:16], AF.Exp,
                                       bias=sct2[:, 0:1]))
            exmk = s_ac.n

            if lg_d is not None:
                @blk.gpsimd
                def _(gp, q=q, exmk=exmk):
                    gp.wait_ge(s_ac.h, exmk)
                    s_sp.dma(gp.dma_start(
                        out=lg_d[q * 128:(q + 1) * 128, :],
                        in_=lgt[:, 0:16]))
                    gp.wait_ge(s_sp.h, s_sp.n)

            @blk.vector
            def _(ve, q=q, exmk=exmk):
                ve.wait_ge(s_ac.h, exmk)
                ex = lgt[:, 32:48]
                ws = lgt[:, 48:64]
                m1 = sct2[:, 0:1]
                ve.reduce_sum(m1, ex, AX.X)
                ve.drain()
                ve.reciprocal(m1, m1)
                ve.drain()
                ve.tensor_single_scalar(ex, ex, m1, ALU.mult)
                ve.drain()
                ve.tensor_tensor(ws, ex, ws, ALU.mult)
                ve.drain()
                ve.reduce_sum(m1, ws, AX.X)
                ve.drain()
                ve.reciprocal(m1, m1)
                ve.drain()
                ve.tensor_single_scalar(combine[:, q * 16:(q + 1) * 16],
                                        ws, m1, ALU.mult)
                s_ve.inc(ve.engine_nop())

    with nc.Block() as blk:
        if cmb_d is not None:
            @blk.gpsimd
            def _(gp):
                for q in range(NQT):
                    s_sp.dma(gp.dma_start(
                        out=cmb_d[q * 128:(q + 1) * 128, :],
                        in_=combine[:, q * 16:(q + 1) * 16]))
                gp.wait_ge(s_sp.h, s_sp.n)
        if h2t_d is not None:
            @blk.gpsimd
            def _(gp):
                for dc in range(16):
                    s_sp.dma(gp.dma_start(
                        out=h2t_d[dc * 128:(dc + 1) * 128, :],
                        in_=h2_fm[dc][:, :]))
                gp.wait_ge(s_sp.h, s_sp.n)

    A.pop()
    if stop_after == 'P4b':
        return _finish()

    A.push()
    moe = [A.al(128, DM) for _ in range(NQT)]
    A.push()
    hid = [A.al(128, TW) for _ in range(16)]
    wsa = [A.al(128, 256) for _ in range(2)]
    wsb = [A.al(128, 256) for _ in range(2)]
    wdt = [A.al(128, 512) for _ in range(2)]

    for qh in range(2):
        with nc.Block() as blk:
            prev_m = s_mm.n
            for hfp in range(8):
                umrk = []
                for dc in range(16):
                    @blk.scalar
                    def _(sc, hfp=hfp, dc=dc, prev_m=prev_m):
                        if dc >= 2:
                            sc.wait_ge(s_mm.h, umrk[dc - 2])
                        else:
                            sc.wait_ge(s_mm.h, prev_m)
                        s_w.dma(sc.dma_start(
                            out=wsa[dc % 2][:, :],
                            in_=su_d[dc * 128:(dc + 1) * 128,
                                     hfp * 256:(hfp + 1) * 256]))
                        s_w.dma(sc.dma_start(
                            out=wsb[dc % 2][:, :],
                            in_=sg_d[dc * 128:(dc + 1) * 128,
                                     hfp * 256:(hfp + 1) * 256]))
                    wmk = s_w.n

                    @blk.tensor
                    def _(te, qh=qh, dc=dc, wmk=wmk):
                        te.wait_ge(s_w.h, wmk)
                        if dc == 0:
                            te.wait_ge(s_ve.h, s_ve.n)
                            te.wait_ge(s_ac.h, s_ac.n)
                        m = None
                        for i2 in range(2):
                            m = te.matmul(
                                pb[i2][:, :],
                                wsa[dc % 2][:, i2 * 128:(i2 + 1) * 128],
                                h2_fm[dc][:, qh * 512:(qh + 1) * 512],
                                start=(dc == 0), stop=(dc == 15))
                            m = te.matmul(
                                pb[2 + i2][:, :],
                                wsb[dc % 2][:, i2 * 128:(i2 + 1) * 128],
                                h2_fm[dc][:, qh * 512:(qh + 1) * 512],
                                start=(dc == 0), stop=(dc == 15))
                        s_mm.inc(m)
                    umrk.append(s_mm.n)
                prev_m = umrk[-1]

                @blk.scalar
                def _(sc, hfp=hfp, umrk=umrk):
                    sc.wait_ge(s_mm.h, umrk[-1])
                    for i2 in range(2):
                        s_ac.inc(sc.activation(hid[hfp * 2 + i2][:, :],
                                               pb[i2][:, :], AF.Silu))
                simk = s_ac.n

                @blk.vector
                def _(ve, hfp=hfp, simk=simk):
                    ve.wait_ge(s_ac.h, simk)
                    for i2 in range(2):
                        ve.tensor_tensor(hid[hfp * 2 + i2][:, :],
                                         hid[hfp * 2 + i2][:, :],
                                         pb[2 + i2][:, :], ALU.mult)
                    s_ve.inc(ve.engine_nop())

            for fg in range(4):
                sdm = []
                for hfc in range(16):
                    @blk.scalar
                    def _(sc, fg=fg, hfc=hfc, prev_m=prev_m):
                        if hfc >= 2:
                            sc.wait_ge(s_mm.h, sdm[hfc - 2])
                        else:
                            sc.wait_ge(s_mm.h, prev_m)
                        s_w.dma(sc.dma_start(
                            out=wdt[hfc % 2][:, :],
                            in_=sd_d[hfc * 128:(hfc + 1) * 128,
                                     fg * 512:(fg + 1) * 512]))
                    wmk = s_w.n

                    @blk.tensor
                    def _(te, fg=fg, hfc=hfc, wmk=wmk):
                        te.wait_ge(s_w.h, wmk)
                        if hfc == 0:
                            te.wait_ge(s_ve.h, s_ve.n)
                        m = None
                        for qq in range(4):
                            m = te.matmul(
                                pb[4 + qq][:, :],
                                hid[hfc][:, qq * 128:(qq + 1) * 128],
                                wdt[hfc % 2][:, :],
                                start=(hfc == 0), stop=(hfc == 15))
                        s_mm.inc(m)
                    sdm.append(s_mm.n)
                prev_m = sdm[-1]

                @blk.vector
                def _(ve, qh=qh, fg=fg, sdm=sdm):
                    ve.wait_ge(s_mm.h, sdm[-1])
                    for qq in range(4):
                        qidx = qh * 4 + qq
                        ve.tensor_copy(moe[qidx][:, fg * 512:(fg + 1) * 512],
                                       pb[4 + qq][:, :])
                    s_ve.inc(ve.engine_nop())

    for e in range(16):
        with nc.Block() as blk:
            prev_m = s_mm.n
            for qtr in range(4):
                umrk = []
                for dc in range(16):
                    @blk.scalar
                    def _(sc, e=e, qtr=qtr, dc=dc, prev_m=prev_m):
                        if dc >= 2:
                            sc.wait_ge(s_mm.h, umrk[dc - 2])
                        else:
                            sc.wait_ge(s_mm.h, prev_m)
                        s_w.dma(sc.dma_start(
                            out=wsa[dc % 2][:, :],
                            in_=wu_d[e][dc * 128:(dc + 1) * 128,
                                        qtr * 256:(qtr + 1) * 256]))
                        s_w.dma(sc.dma_start(
                            out=wsb[dc % 2][:, :],
                            in_=wg_d[e][dc * 128:(dc + 1) * 128,
                                        qtr * 256:(qtr + 1) * 256]))
                    wmk = s_w.n

                    @blk.tensor
                    def _(te, dc=dc, wmk=wmk):
                        te.wait_ge(s_w.h, wmk)
                        if dc == 0:
                            te.wait_ge(s_ve.h, s_ve.n)
                            te.wait_ge(s_ac.h, s_ac.n)
                        m = None
                        for i2 in range(2):
                            for qh in range(2):
                                m = te.matmul(
                                    pb[i2 * 2 + qh][:, :],
                                    wsa[dc % 2][:, i2 * 128:(i2 + 1) * 128],
                                    h2_fm[dc][:, qh * 512:(qh + 1) * 512],
                                    start=(dc == 0), stop=(dc == 15))
                                m = te.matmul(
                                    pb[4 + i2 * 2 + qh][:, :],
                                    wsb[dc % 2][:, i2 * 128:(i2 + 1) * 128],
                                    h2_fm[dc][:, qh * 512:(qh + 1) * 512],
                                    start=(dc == 0), stop=(dc == 15))
                        s_mm.inc(m)
                    umrk.append(s_mm.n)
                prev_m = umrk[-1]

                @blk.scalar
                def _(sc, qtr=qtr, umrk=umrk):
                    sc.wait_ge(s_mm.h, umrk[-1])
                    for i2 in range(2):
                        for qh in range(2):
                            hf = qtr * 2 + i2
                            s_ac.inc(sc.activation(
                                hid[hf * 2 + qh][:, :],
                                pb[i2 * 2 + qh][:, :], AF.Silu))
                simk = s_ac.n

                @blk.vector
                def _(ve, qtr=qtr, simk=simk):
                    ve.wait_ge(s_ac.h, simk)
                    for i2 in range(2):
                        for qh in range(2):
                            hf = qtr * 2 + i2
                            ve.tensor_tensor(hid[hf * 2 + qh][:, :],
                                             hid[hf * 2 + qh][:, :],
                                             pb[4 + i2 * 2 + qh][:, :],
                                             ALU.mult)
                    s_ve.inc(ve.engine_nop())

            for fg in range(4):
                wdm = []
                for hfc in range(8):
                    @blk.scalar
                    def _(sc, e=e, fg=fg, hfc=hfc, prev_m=prev_m):
                        if hfc >= 2:
                            sc.wait_ge(s_mm.h, wdm[hfc - 2])
                        else:
                            sc.wait_ge(s_mm.h, prev_m)
                        s_w.dma(sc.dma_start(
                            out=wdt[hfc % 2][:, :],
                            in_=wd_d[e][hfc * 128:(hfc + 1) * 128,
                                        fg * 512:(fg + 1) * 512]))
                    wmk = s_w.n

                    @blk.tensor
                    def _(te, fg=fg, hfc=hfc, wmk=wmk):
                        te.wait_ge(s_w.h, wmk)
                        if hfc == 0:
                            te.wait_ge(s_ve.h, s_ve.n)
                        m = None
                        for qq in range(8):
                            qh, ql = qq // 4, qq % 4
                            m = te.matmul(
                                pb[qq][:, :],
                                hid[hfc * 2 + qh][:, ql * 128:(ql + 1) * 128],
                                wdt[hfc % 2][:, :],
                                start=(hfc == 0), stop=(hfc == 7))
                        s_mm.inc(m)
                    wdm.append(s_mm.n)
                prev_m = wdm[-1]

                @blk.vector
                def _(ve, e=e, fg=fg, wdm=wdm):
                    ve.wait_ge(s_mm.h, wdm[-1])
                    for qq in range(8):
                        ve.scalar_tensor_tensor(
                            moe[qq][:, fg * 512:(fg + 1) * 512],
                            pb[qq][:, :], combine[:, qq * 16 + e:
                                                  qq * 16 + e + 1],
                            moe[qq][:, fg * 512:(fg + 1) * 512],
                            ALU.mult, ALU.add)
                    s_ve.inc(ve.engine_nop())

    A.pop()
    A.push()
    gb_bc = A.al(128, DM)
    x2r = A.al(128, DM)
    outt = A.al(128, DM)
    mrow6 = A.al(1, DM)

    with nc.Block() as blk:
        @blk.sync
        def _(sync):
            s_x.dma(sync.dma_start(out=mrow6[:, :],
                                   in_=mod_d[:, 5 * DM:6 * DM]))
        lm6 = s_x.n

        @blk.tensor
        def _(te):
            te.wait_ge(s_x.h, lm6)
            m = None
            for i in range(4):
                m = te.matmul(pb[i][:, :], ones_r[:, :],
                              mrow6[:, i * 512:(i + 1) * 512],
                              start=True, stop=True)
            s_mm.inc(m)
        bm6 = s_mm.n

        @blk.vector
        def _(ve):
            ve.wait_ge(s_mm.h, bm6)
            for i in range(4):
                ve.tensor_copy(gb_bc[:, i * 512:(i + 1) * 512], pb[i][:, :])
            s_ve.inc(ve.engine_nop())

    with nc.Block() as blk:
        prev_out = s_sp.n
        for q in range(NQT):
            @blk.sync
            def _(sync, q=q):
                sync.wait_ge(s_ve.h, s_ve.n)
                s_x.dma(sync.dma_start(out=x2r[:, :],
                                       in_=x2_d[q * 128:(q + 1) * 128, :]))
            xmk = s_x.n

            @blk.vector
            def _(ve, q=q, xmk=xmk, prev_out=prev_out):
                ve.wait_ge(s_x.h, xmk)
                ve.wait_ge(s_sp.h, prev_out)
                ve.tensor_tensor(outt[:, :], moe[q][:, :], gb_bc[:, :],
                                 ALU.mult)
                ve.tensor_tensor(outt[:, :], outt[:, :], x2r[:, :], ALU.add)
                s_ve.inc(ve.engine_nop())
            omk = s_ve.n

            @blk.gpsimd
            def _(gp, q=q, omk=omk):
                gp.wait_ge(s_ve.h, omk)
                s_sp.dma(gp.dma_start(out=out_d[q * 128:(q + 1) * 128, :],
                                      in_=outt[:, :]))
            prev_out = s_sp.n

        @blk.gpsimd
        def _(gp):
            gp.wait_ge(s_sp.h, s_sp.n)

    return nc


def kernel(**inputs):
    x = np.ascontiguousarray(np.asarray(inputs["L0_attn_x"],
                                        dtype=np.float32))[0]
    assert x.shape == (S, DM)
    nc = build_nc()
    ar = np.arange(S, dtype=np.float32)

    base = {
        "wq": np.ascontiguousarray(inputs["L0_attn_Wq"], dtype=np.float32),
        "wkv": np.ascontiguousarray(inputs["L0_attn_Wkv"], dtype=np.float32),
        "wo": np.ascontiguousarray(inputs["L0_attn_Wo"], dtype=np.float32),
        "bo": np.ascontiguousarray(inputs["L0_attn_bo"],
                                   dtype=np.float32).reshape(1, DM),
        "rw": np.ascontiguousarray(inputs["L0_ffn_router_w"],
                                   dtype=np.float32),
        "rb": np.ascontiguousarray(inputs["L0_ffn_router_bias"],
                                   dtype=np.float32).reshape(1, 16),
        "wu": np.ascontiguousarray(inputs["L0_ffn_Wu"], dtype=np.float32),
        "wg": np.ascontiguousarray(inputs["L0_ffn_Wg"], dtype=np.float32),
        "wd": np.ascontiguousarray(inputs["L0_ffn_Wd"], dtype=np.float32),
        "su": np.ascontiguousarray(inputs["L0_ffn_Su"],
                                   dtype=np.float32).reshape(DM, DM),
        "sg": np.ascontiguousarray(inputs["L0_ffn_Sg"],
                                   dtype=np.float32).reshape(DM, DM),
        "sd": np.ascontiguousarray(inputs["L0_ffn_Sd"],
                                   dtype=np.float32).reshape(DM, DM),
        "cvec": np.ascontiguousarray(inputs["L0_c"],
                                     dtype=np.float32).reshape(1, DM),
        "wmod": np.ascontiguousarray(inputs["L0_wmod"], dtype=np.float32),
    }
    in_maps = []
    for c in range(8):
        sl = slice(c * OWN, (c + 1) * OWN)
        perm_x = np.concatenate(
            [x[sl], x[:c * OWN], x[(c + 1) * OWN:]], axis=0)
        perm_p = np.concatenate([ar[sl], ar[:c * OWN], ar[(c + 1) * OWN:]])
        m = dict(base)
        m["x"] = np.ascontiguousarray(perm_x)
        m["tokr"] = np.ascontiguousarray(perm_p.reshape(1, S))
        m["tokc"] = np.ascontiguousarray(perm_p.reshape(64, 128).T)
        in_maps.append(m)

    res = run_bass_kernel_spmd(nc, in_maps, core_ids=list(range(8)))
    out = np.concatenate([np.asarray(r["out"]) for r in res.results], axis=0)
    return out.reshape(1, S, DM).astype(np.float32)


# version 08
# speedup vs baseline: 1.1460x; 1.1460x over previous
import math

import numpy as np
import concourse.bass as bass
import concourse.mybir as mybir
from concourse.bass_utils import run_bass_kernel_spmd

F32 = mybir.dt.float32
BF16 = mybir.dt.bfloat16
I32 = mybir.dt.int32
AF = mybir.ActivationFunctionType
ALU = mybir.AluOpType
AX = mybir.AxisListType

S, DM, OWN, H, KVH, HD = 8192, 2048, 1024, 16, 8, 128
NW, TW = 16, 512
NCH, CH = 8, 1024
NPASS, HPP, KVPP = 4, 4, 2
NQT = 8
HUGE = 1.0e12

SLOPES = [2.0 ** (-(h + 1) / 2.0) for h in range(H)]
B_H = [30.0 - 257.0 * s for s in SLOPES]
SCALE = 1.0 / math.sqrt(float(HD))


def _sp(v):
    hi = float(np.float32(v))
    return hi, float(np.float32(v - hi))


SC_HI, SC_LO = _sp(SCALE)
L64_HI, L64_LO = _sp(-math.log(10000.0) / 64.0)
TWO_PI = 2.0 * math.pi
INV2PI = float(np.float32(1.0 / TWO_PI))
CW1 = 6.28125
CW2 = float(np.float32(TWO_PI - CW1))
CW3 = float(np.float32(TWO_PI - CW1 - CW2))
HPI_HI, HPI_LO = _sp(math.pi / 2.0)
SLOPE_SP = [_sp(s) for s in SLOPES]
FRQ_SP = [_sp(10000.0 ** (-i / 64.0)) for i in range(64)]

ARENA_W = 51500


class Ctr:
    def __init__(self, handle):
        self.h = handle
        self.n = 0

    def inc(self, inst, k=1):
        inst.then_inc(self.h, k)
        self.n += k
        return inst

    def dma(self, inst):
        return self.inc(inst, 16)


class Arena:
    def __init__(self, nc):
        self.t = nc.alloc_sbuf_tensor("arena", [128, ARENA_W], F32)
        self.off = 0
        self.marks = []
        self.peak = 0

    def al(self, parts, n, dtype=F32):
        off = (self.off + 15) & ~15
        self.off = off + n
        self.peak = max(self.peak, self.off)
        assert self.off <= ARENA_W, f"arena overflow: {self.off}"
        ap = self.t[0:parts, off:off + n]
        if dtype is not F32:
            ap = ap.bitcast(dtype)
        return ap

    def push(self):
        self.marks.append(self.off)

    def pop(self):
        self.off = self.marks.pop()


def build_nc(taps=(), stop_after=None, p1_nw=NW, p1_parts=('ln', 'cx', 'tr', 'K', 'V', 'Q')):
    nc = bass.Bass("TRN2", target_bir_lowering=False,
                   detect_race_conditions=False)

    x_d = nc.dram_tensor("x", [S, DM], F32, kind="ExternalInput").ap()
    tokr_d = nc.dram_tensor("tokr", [1, S], F32, kind="ExternalInput").ap()
    tokc_d = nc.dram_tensor("tokc", [128, 64], F32, kind="ExternalInput").ap()
    wq_d = nc.dram_tensor("wq", [DM, DM], F32, kind="ExternalInput").ap()
    wkv_d = nc.dram_tensor("wkv", [DM, DM], F32, kind="ExternalInput").ap()
    wo_d = nc.dram_tensor("wo", [DM, DM], F32, kind="ExternalInput").ap()
    bo_d = nc.dram_tensor("bo", [1, DM], F32, kind="ExternalInput").ap()
    rw_d = nc.dram_tensor("rw", [16, DM], F32, kind="ExternalInput").ap()
    rb_d = nc.dram_tensor("rb", [1, 16], F32, kind="ExternalInput").ap()
    wu_d = nc.dram_tensor("wu", [16, DM, 1024], F32, kind="ExternalInput").ap()
    wg_d = nc.dram_tensor("wg", [16, DM, 1024], F32, kind="ExternalInput").ap()
    wd_d = nc.dram_tensor("wd", [16, 1024, DM], F32, kind="ExternalInput").ap()
    su_d = nc.dram_tensor("su", [DM, DM], F32, kind="ExternalInput").ap()
    sg_d = nc.dram_tensor("sg", [DM, DM], F32, kind="ExternalInput").ap()
    sd_d = nc.dram_tensor("sd", [DM, DM], F32, kind="ExternalInput").ap()
    c_d = nc.dram_tensor("cvec", [1, DM], F32, kind="ExternalInput").ap()
    wmod_d = nc.dram_tensor("wmod", [DM, 6 * DM], F32,
                            kind="ExternalInput").ap()
    out_d = nc.dram_tensor("out", [OWN, DM], F32, kind="ExternalOutput").ap()

    def scr(name, shape):
        kind = "ExternalOutput" if name in taps else "Internal"
        return nc.dram_tensor(name, shape, F32, kind=kind).ap()

    ks_d = scr("Ks", [KVH * HD, S])
    vs_d = scr("Vs", [S, KVH * HD])
    qs_d = scr("Qs", [DM, OWN])
    x2_d = scr("X2s", [OWN, DM])
    mod_d = scr("MODs", [1, 6 * DM])
    acc_d = scr("ACCs", [OWN, 16 * 130]) if "ACCs" in taps else None
    hs_d = scr("Hs", [S, DM]) if "Hs" in taps else None
    h2t_d = scr("H2s", [DM, OWN]) if "H2s" in taps else None
    lg_d = scr("LGs", [OWN, 16]) if "LGs" in taps else None
    cmb_d = scr("CMBs", [OWN, 16]) if "CMBs" in taps else None

    pb = [nc.alloc_psum_tensor(f"pb{i}", [128, 512], F32) for i in range(8)]

    A = Arena(nc)
    ident = A.al(128, 128)
    ones_r = A.al(1, 128)
    ones_c = A.al(128, 1)
    c_fm = A.al(128, 16)
    s_fm = A.al(128, 16)
    kp_adj = A.al(128, 64)
    qpa_bc = A.al(128, OWN)
    rwT = A.al(128, 256)
    rwTh = A.al(128, 128, BF16)
    rwTl = A.al(128, 128, BF16)
    rb_bc = A.al(128, 16)
    vsum = A.al(1, KVH * HD)
    vsum_bc = A.al(128, KVH * HD)
    freqc = A.al(64, 1)
    flo = A.al(64, 1)
    bh_col = A.al(128, 16)
    epsc = A.al(128, 1)
    combine = A.al(128, 16 * NQT)

    sem_handles = []
    for nm in ("s_x", "s_w", "s_sp", "s_mm", "s_ve", "s_ac", "s_gp"):
        cm = nc.semaphore(nm)
        sem_handles.append(cm.__enter__())
    s_x, s_w, s_sp, s_mm, s_ve, s_ac, s_gp = [Ctr(h) for h in sem_handles]


    def _finish():
        with nc.Block() as blk:
            @blk.gpsimd
            def _(gp):
                gp.wait_ge(s_sp.h, s_sp.n)
                gp.wait_ge(s_x.h, s_x.n)
                gp.wait_ge(s_w.h, s_w.n)
        return nc
    A.push()
    tokr = A.al(1, S)

    A.push()
    mod6 = A.al(1, 6 * DM)
    mrow = A.al(1, DM)
    wm_sb = [A.al(128, 4096) for _ in range(2)]
    rw_sb = A.al(16, DM)
    rb_sb = A.al(1, 16)
    kp_raw = A.al(128, 64)

    with nc.Block() as blk:
        @blk.sync
        def _(sync):
            s_x.dma(sync.dma_start(out=tokr[:, :], in_=tokr_d[:, :]))
            s_x.dma(sync.dma_start(out=kp_raw[:, :], in_=tokc_d[:, :]))
            s_x.dma(sync.dma_start(
                out=c_fm[:, :],
                in_=c_d.rearrange("o (k p) -> (o p) k", p=128),
                allow_slow_non_contiguous=True))
            s_x.dma(sync.dma_start(out=rw_sb[:, :], in_=rw_d[:, :]))
            s_x.dma(sync.dma_start(out=rb_sb[:, :], in_=rb_d[:, :]))
        ld0 = s_x.n

        @blk.gpsimd
        def _(gp):
            gp.iota(ident[:, :], [[1, 128]], base=0, channel_multiplier=-1,
                    allow_small_or_imprecise_dtypes=True)
            gp.memset(ones_r[:, :], 1.0)
            gp.memset(ones_c[:, :], 1.0)
            gp.memset(epsc[:, :], 1e-5)
            for h in range(16):
                gp.memset(bh_col[:, h:h + 1], -B_H[h])
            for i in range(64):
                gp.memset(mrow[0:1, 1024 + i:1025 + i], FRQ_SP[i][0])
                gp.memset(mrow[0:1, 1088 + i:1089 + i], FRQ_SP[i][1])
            s_gp.inc(gp.engine_nop())
        gp0 = s_gp.n

        @blk.vector
        def _(ve):
            ve.wait_ge(s_gp.h, gp0)
            ve.tensor_single_scalar(ident[:, :], ident[:, :], 0.0,
                                    ALU.is_equal)
            ve.wait_ge(s_x.h, ld0)
            ve.tensor_single_scalar(kp_adj[:, :], kp_raw[:, :], 64.0,
                                    ALU.is_lt)
            ve.drain()
            ve.scalar_tensor_tensor(kp_adj[:, :], kp_adj[:, :], -2.0 * HUGE,
                                    kp_raw[:, :], ALU.mult, ALU.add)
            ve.tensor_single_scalar(mrow[:, 0:OWN], tokr[:, 0:OWN], 64.0,
                                    ALU.is_lt)
            ve.drain()
            ve.scalar_tensor_tensor(mrow[:, 0:OWN], mrow[:, 0:OWN], HUGE,
                                    tokr[:, 0:OWN], ALU.mult, ALU.add)
            s_ve.inc(ve.engine_nop())
        ve0 = s_ve.n

        @blk.scalar
        def _(sc):
            sc.wait_ge(s_ve.h, ve0)
            s_ac.inc(sc.activation(s_fm[:, :], c_fm[:, :], AF.Silu))

        @blk.tensor
        def _(te):
            te.wait_ge(s_ve.h, ve0)
            te.matmul(pb[7][0:64, 0:1], mrow[0:1, 1024:1088],
                      ident[0:1, 0:1], is_transpose=True,
                      start=True, stop=True)
            te.matmul(pb[7][0:64, 1:2], mrow[0:1, 1088:1152],
                      ident[0:1, 0:1], is_transpose=True,
                      start=True, stop=True)
            for i in range(2):
                te.matmul(pb[i][:, :], ones_r[:, :],
                          mrow[:, i * 512:(i + 1) * 512],
                          start=True, stop=True)
            m = te.matmul(pb[2][:, 0:16], ones_r[:, :], rb_sb[:, :],
                          start=True, stop=True)
            s_mm.inc(m)
        mm0 = s_mm.n

        @blk.vector
        def _(ve):
            ve.wait_ge(s_mm.h, mm0)
            ve.tensor_copy(freqc[:, :], pb[7][0:64, 0:1])
            ve.tensor_copy(flo[:, :], pb[7][0:64, 1:2])
            for i in range(2):
                ve.tensor_copy(qpa_bc[:, i * 512:(i + 1) * 512], pb[i][:, :])
            ve.tensor_copy(rb_bc[:, :], pb[2][:, 0:16])
            s_ve.inc(ve.engine_nop())

        for g in range(4):
            @blk.tensor
            def _(te, g=g):
                te.wait_ge(s_ve.h, s_ve.n)
                m = None
                for j in range(4):
                    dc = g * 4 + j
                    m = te.matmul(pb[3 + j][:, 0:16],
                                  rw_sb[:, dc * 128:(dc + 1) * 128],
                                  ident[0:16, 0:16], is_transpose=True,
                                  start=True, stop=True)
                s_mm.inc(m)
            gmk = s_mm.n

            @blk.vector
            def _(ve, g=g, gmk=gmk):
                ve.wait_ge(s_mm.h, gmk)
                for j in range(4):
                    dc = g * 4 + j
                    ve.tensor_copy(rwT[:, dc * 16:(dc + 1) * 16],
                                   pb[3 + j][:, 0:16])
                if g == 3:
                    ve.tensor_copy(rwTh[:, :], rwT[:, :])
                    ve.drain()
                    ve.tensor_tensor(rwTl[:, :], rwT[:, :], rwTh[:, :],
                                     ALU.subtract)
                s_ve.inc(ve.engine_nop())

    with nc.Block() as blk:
        mm_marks = []
        for g in range(3):
            for dc in range(16):
                @blk.scalar
                def _(sc, g=g, dc=dc):
                    idx = g * 16 + dc
                    if idx >= 2:
                        sc.wait_ge(s_mm.h, mm_marks[idx - 2])
                    s_w.dma(sc.dma_start(
                        out=wm_sb[idx % 2][:, :],
                        in_=wmod_d[dc * 128:(dc + 1) * 128,
                                   g * 4096:(g + 1) * 4096]))
                dmark = s_w.n

                @blk.tensor
                def _(te, g=g, dc=dc, dmark=dmark):
                    te.wait_ge(s_w.h, dmark)
                    if dc == 0:
                        te.wait_ge(s_ve.h, s_ve.n)
                    m = None
                    for jt in range(8):
                        m = te.matmul(
                            pb[jt][0:1, :], s_fm[:, dc:dc + 1],
                            wm_sb[(g * 16 + dc) % 2][:,
                                                     jt * 512:(jt + 1) * 512],
                            start=(dc == 0), stop=(dc == 15))
                    s_mm.inc(m)
                mm_marks.append(s_mm.n)

            @blk.vector
            def _(ve, g=g):
                ve.wait_ge(s_mm.h, mm_marks[-1])
                for jt in range(8):
                    ve.tensor_copy(
                        mod6[:, g * 4096 + jt * 512:g * 4096 + (jt + 1) * 512],
                        pb[jt][0:1, :])
                s_ve.inc(ve.engine_nop())

        @blk.vector
        def _(ve):
            for k in (1, 4):
                sl = slice(k * DM, (k + 1) * DM)
                ve.tensor_single_scalar(mod6[:, sl], mod6[:, sl], 1.0,
                                        ALU.add)
            for k in (2, 5):
                sl = slice(k * DM, (k + 1) * DM)
                ve.tensor_single_scalar(mrow[:, :], mod6[:, sl], 0.0,
                                        ALU.is_ge)
                ve.drain()
                ve.tensor_single_scalar(mrow[:, :], mrow[:, :], 2.0,
                                        ALU.mult)
                ve.drain()
                ve.tensor_single_scalar(mrow[:, :], mrow[:, :], -1.0,
                                        ALU.add)
                ve.drain()
                ve.tensor_tensor(mod6[:, sl], mod6[:, sl], mrow[:, :],
                                 ALU.mult)
                ve.drain()
                ve.tensor_single_scalar(mod6[:, sl], mod6[:, sl], 0.25,
                                        ALU.max)
                ve.drain()
                ve.tensor_tensor(mod6[:, sl], mod6[:, sl], mrow[:, :],
                                 ALU.mult)
                ve.drain()
            s_ve.inc(ve.engine_nop())
        vem = s_ve.n

        @blk.gpsimd
        def _(gp):
            gp.wait_ge(s_ve.h, vem)
            s_sp.dma(gp.dma_start(out=mod_d[:, :], in_=mod6[:, :]))
            gp.wait_ge(s_sp.h, s_sp.n)

    A.pop()
    if stop_after == 'P0':
        return _finish()

    A.push()
    ca1_bc = A.al(128, DM)
    sa_bc = A.al(128, DM)

    A.push()
    mrow2 = A.al(1, 2 * DM)
    with nc.Block() as blk:
        @blk.sync
        def _(sync):
            s_x.dma(sync.dma_start(out=mrow2[:, 0:DM],
                                   in_=mod_d[:, DM:2 * DM]))
            s_x.dma(sync.dma_start(out=mrow2[:, DM:2 * DM],
                                   in_=mod_d[:, 0:DM]))
        ldm = s_x.n

        @blk.tensor
        def _(te):
            te.wait_ge(s_x.h, ldm)
            m = None
            for i in range(4):
                te.matmul(pb[i][:, :], ones_r[:, :],
                          mrow2[:, i * 512:(i + 1) * 512],
                          start=True, stop=True)
                m = te.matmul(pb[4 + i][:, :], ones_r[:, :],
                              mrow2[:, DM + i * 512:DM + (i + 1) * 512],
                              start=True, stop=True)
            s_mm.inc(m)
        mmb = s_mm.n

        @blk.vector
        def _(ve):
            ve.wait_ge(s_mm.h, mmb)
            for i in range(4):
                ve.tensor_copy(ca1_bc[:, i * 512:(i + 1) * 512], pb[i][:, :])
                ve.tensor_copy(sa_bc[:, i * 512:(i + 1) * 512],
                               pb[4 + i][:, :])
            s_ve.inc(ve.engine_nop())
    A.pop()

    for w in range(p1_nw):
        own = w < 2
        A.push()
        x_sb = A.al(128, DM)
        h_tm = A.al(128, DM)
        h_fm = [A.al(128, TW) for _ in range(16)]
        sct = A.al(128, 8)
        posw = A.al(64, TW)
        t_ang = A.al(64, TW)
        t_sin = A.al(64, TW)
        t_cos = A.al(64, TW)
        t_tmp = A.al(64, TW)
        t_nf = A.al(64, TW)
        t_sq = A.al(64, TW)
        t_cq = A.al(64, TW)
        tabi = A.al(64, TW, I32)
        rsc1 = A.al(64, TW)
        rsc2 = A.al(64, TW)
        wstr = [A.al(128, 1024) for _ in range(2)]
        ksb = A.al(128, 8 * TW)
        vsb = [A.al(128, 1024) for _ in range(4)]
        qsb = A.al(128, 8 * TW) if own else None
        cxbc = A.al(128, TW) if own else None
        cxrow = A.al(1, TW) if own else None

        with nc.Block() as blk:
            @blk.tensor
            def _(te):
                m = te.matmul(pb[0][0:64, :], ones_r[:, 0:64],
                              tokr[:, w * TW:(w + 1) * TW],
                              start=True, stop=True)
                s_mm.inc(m)
            posmk = s_mm.n

            @blk.vector
            def _(ve):
                ve.wait_ge(s_mm.h, posmk)
                ve.wait_ge(s_sp.h, s_sp.n)
                ve.tensor_copy(posw[:, :], pb[0][0:64, :])
                ve.tensor_scalar(t_ang[:, :], posw[:, :], freqc[:, 0:1],
                                 None, ALU.mult)
                ve.tensor_single_scalar(t_tmp[:, :], t_ang[:, :], INV2PI,
                                        ALU.mult)
                ve.tensor_copy(tabi[:, :], t_tmp[:, :])
                ve.tensor_copy(t_nf[:, :], tabi[:, :])
                ve.scalar_tensor_tensor(t_tmp[:, :], t_nf[:, :], -CW1,
                                        t_ang[:, :], ALU.mult, ALU.add)
                ve.scalar_tensor_tensor(t_tmp[:, :], t_nf[:, :], -CW2,
                                        t_tmp[:, :], ALU.mult, ALU.add)
                ve.scalar_tensor_tensor(t_sin[:, :], t_nf[:, :], -CW3,
                                        t_tmp[:, :], ALU.mult, ALU.add)
                ve.drain()
                ve.scalar_tensor_tensor(t_sin[:, :], posw[:, :], flo[:, 0:1],
                                        t_sin[:, :], ALU.mult, ALU.add)
                ve.tensor_single_scalar(t_tmp[:, :], t_ang[:, :], INV2PI,
                                        ALU.mult)
                ve.tensor_single_scalar(t_tmp[:, :], t_tmp[:, :], 0.25,
                                        ALU.add)
                ve.tensor_copy(tabi[:, :], t_tmp[:, :])
                ve.tensor_copy(t_nf[:, :], tabi[:, :])
                ve.scalar_tensor_tensor(t_tmp[:, :], t_nf[:, :], -CW1,
                                        t_ang[:, :], ALU.mult, ALU.add)
                ve.scalar_tensor_tensor(t_tmp[:, :], t_nf[:, :], -CW2,
                                        t_tmp[:, :], ALU.mult, ALU.add)
                ve.scalar_tensor_tensor(t_tmp[:, :], t_nf[:, :], -CW3,
                                        t_tmp[:, :], ALU.mult, ALU.add)
                ve.tensor_single_scalar(t_tmp[:, :], t_tmp[:, :], HPI_HI,
                                        ALU.add)
                ve.drain()
                ve.tensor_single_scalar(t_cos[:, :], t_tmp[:, :], HPI_LO,
                                        ALU.add)
                ve.drain()
                ve.scalar_tensor_tensor(t_cos[:, :], posw[:, :], flo[:, 0:1],
                                        t_cos[:, :], ALU.mult, ALU.add)
                s_ve.inc(ve.engine_nop())
            tabmk = s_ve.n

            @blk.scalar
            def _(sc):
                sc.wait_ge(s_ve.h, tabmk)
                sc.activation(t_sin[:, :], t_sin[:, :], AF.Sin)
                s_ac.inc(sc.activation(t_cos[:, :], t_cos[:, :], AF.Sin))
            tabac = s_ac.n

            if own:
                @blk.vector
                def _(ve):
                    ve.wait_ge(s_ac.h, tabac)
                    ve.tensor_single_scalar(t_sq[:, :], t_sin[:, :], SC_HI,
                                            ALU.mult)
                    ve.scalar_tensor_tensor(t_sq[:, :], t_sin[:, :], SC_LO,
                                            t_sq[:, :], ALU.mult, ALU.add)
                    ve.tensor_single_scalar(t_cq[:, :], t_cos[:, :], SC_HI,
                                            ALU.mult)
                    ve.scalar_tensor_tensor(t_cq[:, :], t_cos[:, :], SC_LO,
                                            t_cq[:, :], ALU.mult, ALU.add)
                    s_ve.inc(ve.engine_nop())

            for tt in range(4):
                row0 = w * TW + tt * 128

                if 'ln' in p1_parts:
                    @blk.sync
                    def _(sync, row0=row0):
                        sync.wait_ge(s_mm.h, s_mm.n)
                        s_x.dma(sync.dma_start(out=x_sb[:, :],
                                               in_=x_d[row0:row0 + 128, :]))
                    xmk = s_x.n

                    @blk.scalar
                    def _(sc, xmk=xmk):
                        sc.wait_ge(s_x.h, xmk)
                        sc.wait_ge(s_ve.h, s_ve.n)
                        s_ac.inc(sc.activation(h_tm[:, :], x_sb[:, :], AF.Square,
                                               accum_out=sct[:, 1:2]))
                    sqmk = s_ac.n

                    @blk.vector
                    def _(ve, xmk=xmk, sqmk=sqmk):
                        ve.wait_ge(s_x.h, xmk)
                        ve.reduce_sum(sct[:, 0:1], x_sb[:, :], AX.X)
                        ve.wait_ge(s_ac.h, sqmk)
                        ve.tensor_single_scalar(sct[:, 2:3], sct[:, 0:1],
                                                1.0 / DM, ALU.mult)
                        ve.drain()
                        ve.tensor_tensor(sct[:, 3:4], sct[:, 2:3], sct[:, 2:3],
                                         ALU.mult)
                        ve.tensor_single_scalar(sct[:, 4:5], sct[:, 1:2],
                                                1.0 / DM, ALU.mult)
                        ve.drain()
                        ve.tensor_tensor(sct[:, 4:5], sct[:, 4:5], sct[:, 3:4],
                                         ALU.subtract)
                        ve.drain()
                        s_ve.inc(ve.engine_nop())
                    stmk = s_ve.n

                    @blk.scalar
                    def _(sc, stmk=stmk):
                        sc.wait_ge(s_ve.h, stmk)
                        s_ac.inc(sc.activation(sct[:, 5:6], sct[:, 4:5], AF.Sqrt,
                                               bias=epsc[:, 0:1]))
                    sdmk = s_ac.n

                    @blk.vector
                    def _(ve, sdmk=sdmk, own=own):
                        ve.wait_ge(s_ac.h, sdmk)
                        ve.reciprocal(sct[:, 6:7], sct[:, 5:6])
                        ve.drain()
                        ve.tensor_scalar(h_tm[:, :], x_sb[:, :], sct[:, 2:3],
                                         sct[:, 6:7], ALU.subtract, ALU.mult)
                        ve.tensor_tensor(h_tm[:, :], h_tm[:, :], ca1_bc[:, :],
                                         ALU.mult)
                        ve.tensor_tensor(h_tm[:, :], h_tm[:, :], sa_bc[:, :],
                                         ALU.add)
                        if own:
                            ve.reduce_sum(sct[:, 7:8], h_tm[:, :], AX.X,
                                          apply_absolute_value=True)
                        s_ve.inc(ve.engine_nop())
                    hmk = s_ve.n
                    if hs_d is not None:
                        @blk.gpsimd
                        def _(gp, row0=row0, hmk=hmk):
                            gp.wait_ge(s_ve.h, hmk)
                            s_sp.dma(gp.dma_start(
                                out=hs_d[row0:row0 + 128, :],
                                in_=h_tm[:, :]))
                            gp.wait_ge(s_sp.h, s_sp.n)
                else:
                    hmk = s_ve.n

                if own and 'cx' in p1_parts:
                    @blk.scalar
                    def _(sc, hmk=hmk):
                        sc.wait_ge(s_ve.h, hmk)
                        s_ac.inc(sc.activation(sct[:, 7:8], sct[:, 7:8],
                                               AF.Sigmoid))
                    cxmk = s_ac.n

                    @blk.tensor
                    def _(te, cxmk=cxmk):
                        te.wait_ge(s_ac.h, cxmk)
                        m = te.matmul(pb[0][0:1, 0:128], sct[:, 7:8],
                                      ident[:, :], is_transpose=True,
                                      start=True, stop=True)
                        s_mm.inc(m)
                    cxt = s_mm.n

                    @blk.vector
                    def _(ve, tt=tt, cxt=cxt):
                        ve.wait_ge(s_mm.h, cxt)
                        ve.tensor_copy(cxrow[:, tt * 128:(tt + 1) * 128],
                                       pb[0][0:1, 0:128])
                        s_ve.inc(ve.engine_nop())

                if 'tr' in p1_parts:
                    for half in range(2):
                        @blk.tensor
                        def _(te, half=half):
                            te.wait_ge(s_ve.h, s_ve.n)
                            m = None
                            for j in range(8):
                                dc = half * 8 + j
                                m = te.matmul(pb[j][:, 0:128],
                                              h_tm[:, dc * 128:(dc + 1) * 128],
                                              ident[:, :], is_transpose=True,
                                              start=True, stop=True)
                            s_mm.inc(m)
                        tmk = s_mm.n

                        @blk.vector
                        def _(ve, half=half, tt=tt, tmk=tmk):
                            ve.wait_ge(s_mm.h, tmk)
                            for j in range(8):
                                dc = half * 8 + j
                                ve.tensor_copy(
                                    h_fm[dc][:, tt * 128:(tt + 1) * 128],
                                    pb[j][:, 0:128])
                            s_ve.inc(ve.engine_nop())

            tr_done = s_ve.n
            if own:
                @blk.tensor
                def _(te):
                    te.wait_ge(s_ve.h, tr_done)
                    m = te.matmul(pb[0][:, :], ones_r[:, :], cxrow[:, :],
                                  start=True, stop=True)
                    s_mm.inc(m)
                cbt = s_mm.n

                @blk.vector
                def _(ve, cbt=cbt):
                    ve.wait_ge(s_mm.h, cbt)
                    ve.tensor_copy(cxbc[:, :], pb[0][:, :])
                    s_ve.inc(ve.engine_nop())
                tr_done = s_ve.n

            if 'K' in p1_parts:
                kmarks = []
                for dc in range(16):
                    @blk.scalar
                    def _(sc, dc=dc):
                        if dc >= 2:
                            sc.wait_ge(s_mm.h, kmarks[dc - 2])
                        s_w.dma(sc.dma_start(
                            out=wstr[dc % 2][:, :],
                            in_=wkv_d[dc * 128:(dc + 1) * 128, 0:1024]))
                    wmk = s_w.n

                    @blk.tensor
                    def _(te, dc=dc, wmk=wmk):
                        te.wait_ge(s_w.h, wmk)
                        if dc == 0:
                            te.wait_ge(s_ve.h, tr_done)
                        m = None
                        for kf in range(8):
                            m = te.matmul(pb[kf][:, :],
                                          wstr[dc % 2][:, kf * 128:(kf + 1) * 128],
                                          h_fm[dc][:, :],
                                          start=(dc == 0), stop=(dc == 15))
                        s_mm.inc(m)
                    kmarks.append(s_mm.n)

                @blk.vector
                def _(ve):
                    ve.wait_ge(s_mm.h, kmarks[-1])
                    ve.wait_ge(s_ac.h, tabac)
                    for kf in range(8):
                        dst = ksb[:, kf * TW:(kf + 1) * TW]
                        ve.tensor_tensor(rsc1[:, :], pb[kf][0:64, :], t_cos[:, :],
                                         ALU.mult)
                        ve.tensor_tensor(rsc2[:, :], pb[kf][64:128, :],
                                         t_sin[:, :], ALU.mult)
                        ve.tensor_tensor(dst[0:64, :], rsc1[:, :], rsc2[:, :],
                                         ALU.subtract)
                        ve.tensor_tensor(rsc1[:, :], pb[kf][0:64, :], t_sin[:, :],
                                         ALU.mult)
                        ve.tensor_tensor(rsc2[:, :], pb[kf][64:128, :],
                                         t_cos[:, :], ALU.mult)
                        ve.tensor_tensor(dst[64:128, :], rsc1[:, :], rsc2[:, :],
                                         ALU.add)
                    s_ve.inc(ve.engine_nop())
                kspmk = s_ve.n

                @blk.gpsimd
                def _(gp):
                    gp.wait_ge(s_ve.h, kspmk)
                    for kf in range(8):
                        s_sp.dma(gp.dma_start(
                            out=ks_d[kf * 128:(kf + 1) * 128, w * TW:(w + 1) * TW],
                            in_=ksb[:, kf * TW:(kf + 1) * TW]))

            else:
                kspmk = s_ve.n
            if 'V' in p1_parts:
                vmarks = []
                for dc in range(16):
                    @blk.scalar
                    def _(sc, dc=dc):
                        if dc >= 2:
                            sc.wait_ge(s_mm.h, vmarks[dc - 2])
                        else:
                            sc.wait_ge(s_mm.h, kmarks[-1])
                        s_w.dma(sc.dma_start(
                            out=wstr[dc % 2][:, :],
                            in_=wkv_d[dc * 128:(dc + 1) * 128, 1024:2048]))
                    wmk = s_w.n

                    @blk.tensor
                    def _(te, dc=dc, wmk=wmk):
                        te.wait_ge(s_w.h, wmk)
                        if dc == 0:
                            te.wait_ge(s_ve.h, kspmk)
                        m = None
                        for tt in range(4):
                            for fh in range(2):
                                m = te.matmul(
                                    pb[tt * 2 + fh][:, :],
                                    h_fm[dc][:, tt * 128:(tt + 1) * 128],
                                    wstr[dc % 2][:, fh * 512:(fh + 1) * 512],
                                    start=(dc == 0), stop=(dc == 15))
                        s_mm.inc(m)
                    vmarks.append(s_mm.n)

                @blk.vector
                def _(ve):
                    ve.wait_ge(s_mm.h, vmarks[-1])
                    for tt in range(4):
                        for fh in range(2):
                            ve.tensor_copy(vsb[tt][:, fh * 512:(fh + 1) * 512],
                                           pb[tt * 2 + fh][:, :])
                    s_ve.inc(ve.engine_nop())
                vspmk = s_ve.n

                @blk.gpsimd
                def _(gp):
                    gp.wait_ge(s_ve.h, vspmk)
                    for tt in range(4):
                        r0 = w * TW + tt * 128
                        s_sp.dma(gp.dma_start(out=vs_d[r0:r0 + 128, :],
                                              in_=vsb[tt][:, :]))

            if 'Q' in p1_parts:
                if own:
                    prev_q = vmarks[-1]
                    for rnd in range(2):
                        qmarks = []
                        for dc in range(16):
                            @blk.scalar
                            def _(sc, rnd=rnd, dc=dc, prev_q=prev_q):
                                if dc >= 2:
                                    sc.wait_ge(s_mm.h, qmarks[dc - 2])
                                else:
                                    sc.wait_ge(s_mm.h, prev_q)
                                s_w.dma(sc.dma_start(
                                    out=wstr[dc % 2][:, :],
                                    in_=wq_d[dc * 128:(dc + 1) * 128,
                                             rnd * 1024:(rnd + 1) * 1024]))
                            wmk = s_w.n

                            @blk.tensor
                            def _(te, dc=dc, wmk=wmk):
                                te.wait_ge(s_w.h, wmk)
                                if dc == 0:
                                    te.wait_ge(s_ve.h, s_ve.n)
                                m = None
                                for qf in range(8):
                                    m = te.matmul(
                                        pb[qf][:, :],
                                        wstr[dc % 2][:, qf * 128:(qf + 1) * 128],
                                        h_fm[dc][:, :],
                                        start=(dc == 0), stop=(dc == 15))
                                s_mm.inc(m)
                            qmarks.append(s_mm.n)

                        @blk.vector
                        def _(ve, qmarks=qmarks):
                            ve.wait_ge(s_mm.h, qmarks[-1])
                            for qf in range(8):
                                dst = qsb[:, qf * TW:(qf + 1) * TW]
                                ve.tensor_tensor(rsc1[:, :], pb[qf][0:64, :],
                                                 t_cq[:, :], ALU.mult)
                                ve.tensor_tensor(rsc2[:, :], pb[qf][64:128, :],
                                                 t_sq[:, :], ALU.mult)
                                ve.tensor_tensor(dst[0:64, :], rsc1[:, :],
                                                 rsc2[:, :], ALU.subtract)
                                ve.tensor_tensor(rsc1[:, :], pb[qf][0:64, :],
                                                 t_sq[:, :], ALU.mult)
                                ve.tensor_tensor(rsc2[:, :], pb[qf][64:128, :],
                                                 t_cq[:, :], ALU.mult)
                                ve.tensor_tensor(dst[64:128, :], rsc1[:, :],
                                                 rsc2[:, :], ALU.add)
                                ve.tensor_tensor(dst[0:64, :], dst[0:64, :],
                                                 cxbc[0:64, :], ALU.mult)
                                ve.tensor_tensor(dst[64:128, :], dst[64:128, :],
                                                 cxbc[64:128, :], ALU.mult)
                            s_ve.inc(ve.engine_nop())
                        qspmk = s_ve.n

                        prev_q = qmarks[-1]

                        @blk.gpsimd
                        def _(gp, rnd=rnd, qspmk=qspmk):
                            gp.wait_ge(s_ve.h, qspmk)
                            for qf in range(8):
                                gqf = rnd * 8 + qf
                                s_sp.dma(gp.dma_start(
                                    out=qs_d[gqf * 128:(gqf + 1) * 128,
                                             w * TW:(w + 1) * TW],
                                    in_=qsb[:, qf * TW:(qf + 1) * TW]))
        A.pop()

    A.pop()
    A.pop()
    if stop_after == 'P1':
        return _finish()

    A.push()
    attnacc = [A.al(128, 16 * 130) for _ in range(NQT)]

    A.push()
    q_at = [A.al(128, OWN) for _ in range(HPP)]
    k_at = [A.al(128, CH) for _ in range(KVPP)]
    v_at = [A.al(128, KVPP * 130) for _ in range(8)]
    dist = [A.al(128, 512) for _ in range(16)]
    pt_sb = [A.al(128, 512) for _ in range(8)]
    ssc = [A.al(128, 512) for _ in range(2)]
    dsc = A.al(128, 512)

    with nc.Block() as blk:
        @blk.gpsimd
        def _(gp):
            for q in range(NQT):
                gp.memset(attnacc[q][:, :], 0.0)
            gp.memset(vsum[:, :], 0.0)
            gp.wait_ge(s_sp.h, s_sp.n)
            s_gp.inc(gp.engine_nop())

    for hp in range(NPASS):
        with nc.Block() as blk:
            @blk.sync
            def _(sync):
                for hl in range(HPP):
                    gh = hp * HPP + hl
                    s_x.dma(sync.dma_start(
                        out=q_at[hl][:, :],
                        in_=qs_d[gh * 128:(gh + 1) * 128, :]))

        for chk in range(NCH):
            with nc.Block() as blk:
                @blk.sync
                def _(sync):
                    for kvl in range(KVPP):
                        gkv = hp * KVPP + kvl
                        s_x.dma(sync.dma_start(
                            out=k_at[kvl][:, :],
                            in_=ks_d[gkv * 128:(gkv + 1) * 128,
                                     chk * CH:(chk + 1) * CH]))
                    for kt in range(8):
                        r0 = chk * CH + kt * 128
                        dst = v_at[kt].rearrange(
                            "k (b f) -> k b f", f=130)[:, :, 0:128]
                        src = vs_d[r0:r0 + 128,
                                   hp * 256:(hp + 1) * 256].rearrange(
                            "k (b f) -> k b f", b=KVPP)
                        s_x.dma(sync.dma_start(out=dst, in_=src))
                ldmk = s_x.n

                @blk.gpsimd
                def _(gp):
                    gp.wait_ge(s_x.h, ldmk)
                    for kt in range(8):
                        ones_ap = v_at[kt].rearrange(
                            "k (b f) -> k b f", f=130)[:, :, 128:129]
                        gp.memset(ones_ap, 1.0)
                    s_gp.inc(gp.engine_nop())
                vone = s_gp.n

                @blk.tensor
                def _(te):
                    te.wait_ge(s_x.h, ldmk)
                    te.wait_ge(s_gp.h, vone)
                    m = None
                    for kvl in range(KVPP):
                        for kt in range(8):
                            m = te.matmul(
                                pb[7][0:1, kvl * 128:(kvl + 1) * 128],
                                ones_c[:, :],
                                v_at[kt][:, kvl * 130:kvl * 130 + 128],
                                start=(kt == 0), stop=(kt == 7))
                    s_mm.inc(m)
                vsmk = s_mm.n

                @blk.vector
                def _(ve, hp=hp, vsmk=vsmk):
                    ve.wait_ge(s_mm.h, vsmk)
                    sl = slice(hp * 256, (hp + 1) * 256)
                    ve.tensor_tensor(vsum[:, sl], vsum[:, sl],
                                     pb[7][0:1, 0:256], ALU.add)
                    s_ve.inc(ve.engine_nop())

                bank_free = [s_ve.n, s_ve.n]
                ssc_free = [s_ac.n, s_ac.n]
                pv_bank_free = [s_ve.n] * 4
                pt_free = s_mm.n
                for qt in range(2):
                    @blk.vector
                    def _(ve, qt=qt, chk=chk):
                        for kt in range(8):
                            d = dist[qt * 8 + kt]
                            ve.tensor_single_scalar(
                                d[:, :], qpa_bc[:, qt * 512:(qt + 1) * 512],
                                kp_adj[:, chk * 8 + kt:chk * 8 + kt + 1],
                                ALU.subtract)
                            ve.tensor_single_scalar(dsc[:, :], d[:, :],
                                                    -1.0, ALU.mult)
                            ve.tensor_tensor(d[:, :], d[:, :], dsc[:, :],
                                             ALU.max)
                            ve.tensor_single_scalar(dsc[:, :], d[:, :],
                                                    256.5, ALU.is_le)
                            ve.scalar_tensor_tensor(d[:, :], dsc[:, :],
                                                    HUGE, d[:, :],
                                                    ALU.mult, ALU.add)
                        s_ve.inc(ve.engine_nop())

                    for hl in range(HPP):
                        gh = hp * HPP + hl
                        kvl = hl // 2
                        shi, slo = SLOPE_SP[gh]
                        mm0, ve0, ac0 = s_mm.n, s_ve.n, s_ac.n
                        mm_mark = [mm0 + kt + 1 for kt in range(8)]
                        pv_mark = [mm0 + 9 + qq for qq in range(4)]
                        al_mark = [ve0 + kt + 1 for kt in range(8)]
                        ev_mark = [ve0 + 9 + qq for qq in range(4)]
                        ex_mark = [ac0 + kt + 1 for kt in range(8)]
                        bf_in = list(bank_free)
                        sf_in = list(ssc_free)
                        pvb_in = list(pv_bank_free)
                        ptf_in = pt_free

                        @blk.tensor
                        def _(te, hl=hl, kvl=kvl, qt=qt, bf_in=bf_in,
                              pvb_in=pvb_in, al_mark=al_mark,
                              ex_mark=ex_mark):
                            for kt in range(8):
                                te.wait_ge(s_ve.h,
                                           al_mark[kt - 2] if kt >= 2
                                           else bf_in[kt % 2])
                                m = te.matmul(
                                    pb[kt % 2][:, :],
                                    k_at[kvl][:, kt * 128:(kt + 1) * 128],
                                    q_at[hl][:, qt * 512:(qt + 1) * 512],
                                    start=True, stop=True)
                                s_mm.inc(m)
                            for qq in range(4):
                                if qq == 0:
                                    te.wait_ge(s_ac.h, ex_mark[7])
                                te.wait_ge(s_ve.h, pvb_in[qq])
                                m = None
                                for kt in range(8):
                                    m = te.matmul(
                                        pb[2 + qq][:, 0:129],
                                        pt_sb[kt][:, qq * 128:(qq + 1) * 128],
                                        v_at[kt][:, kvl * 130:kvl * 130 + 129],
                                        start=(kt == 0), stop=(kt == 7))
                                s_mm.inc(m)

                        @blk.vector
                        def _(ve, gh=gh, qt=qt, shi=shi, slo=slo,
                              mm_mark=mm_mark, pv_mark=pv_mark,
                              sf_in=sf_in):
                            for kt in range(8):
                                ve.wait_ge(s_mm.h, mm_mark[kt])
                                ve.wait_ge(s_ac.h,
                                           ex_mark[kt - 2] if kt >= 2
                                           else sf_in[kt % 2])
                                d = dist[qt * 8 + kt]
                                sb = ssc[kt % 2]
                                st = ve.scalar_tensor_tensor(
                                    sb[:, :], d[:, :], -shi,
                                    pb[kt % 2][:, :], ALU.mult, ALU.add)
                                if slo != 0.0:
                                    st = ve.scalar_tensor_tensor(
                                        sb[:, :], d[:, :], -slo, sb[:, :],
                                        ALU.mult, ALU.add)
                                s_ve.inc(st)
                            for qq in range(4):
                                ve.wait_ge(s_mm.h, pv_mark[qq])
                                qidx = qt * 4 + qq
                                a = attnacc[qidx][:, gh * 130:gh * 130 + 129]
                                s_ve.inc(ve.tensor_tensor(
                                    a, a, pb[2 + qq][:, 0:129], ALU.add))

                        @blk.scalar
                        def _(sc, gh=gh, al_mark=al_mark, ptf_in=ptf_in):
                            sc.wait_ge(s_mm.h, ptf_in)
                            for kt in range(8):
                                sc.wait_ge(s_ve.h, al_mark[kt])
                                s_ac.inc(sc.activation(
                                    pt_sb[kt][:, :], ssc[kt % 2][:, :],
                                    AF.Exp, bias=bh_col[:, gh:gh + 1]))

                        assert s_mm.n == mm0 + 12, (s_mm.n, mm0)
                        assert s_ve.n == ve0 + 12, (s_ve.n, ve0)
                        assert s_ac.n == ac0 + 8, (s_ac.n, ac0)
                        bank_free = [al_mark[6], al_mark[7]]
                        ssc_free = [ex_mark[6], ex_mark[7]]
                        pv_bank_free = list(ev_mark)
                        pt_free = pv_mark[3]

    with nc.Block() as blk:
        @blk.tensor
        def _(te):
            m = None
            for i in range(2):
                m = te.matmul(pb[i][:, :], ones_r[:, :],
                              vsum[:, i * 512:(i + 1) * 512],
                              start=True, stop=True)
            s_mm.inc(m)
        vbmk = s_mm.n

        @blk.vector
        def _(ve):
            ve.wait_ge(s_mm.h, vbmk)
            for i in range(2):
                ve.tensor_copy(vsum_bc[:, i * 512:(i + 1) * 512], pb[i][:, :])
            for q in range(NQT):
                for gh in range(16):
                    a = attnacc[q][:, gh * 130:gh * 130 + 128]
                    sm = attnacc[q][:, gh * 130 + 128:gh * 130 + 129]
                    u = ssc[0][:, 0:1]
                    den = ssc[0][:, 1:2]
                    ve.tensor_single_scalar(u, sm, 0.0, ALU.is_equal)
                    ve.drain()
                    ve.scalar_tensor_tensor(den, u, 8192.0, sm,
                                            ALU.mult, ALU.add)
                    ve.drain()
                    ve.reciprocal(den, den)
                    ve.drain()
                    vsl = vsum_bc[:, (gh // 2) * 128:(gh // 2) * 128 + 128]
                    ve.scalar_tensor_tensor(a, vsl, u, a, ALU.mult, ALU.add)
                    ve.tensor_single_scalar(a, a, den, ALU.mult)
            s_ve.inc(ve.engine_nop())
        nomk = s_ve.n

        if acc_d is not None:
            @blk.gpsimd
            def _(gp):
                gp.wait_ge(s_ve.h, nomk)
                for q in range(NQT):
                    s_sp.dma(gp.dma_start(
                        out=acc_d[q * 128:(q + 1) * 128, :],
                        in_=attnacc[q][:, :]))

    A.pop()
    if stop_after == 'P2':
        return _finish()

    A.push()
    bo_bc = A.al(128, DM)
    ga_bc = A.al(128, DM)
    wostr = [A.al(128, 512) for _ in range(2)]
    afm = [A.al(128, 128) for _ in range(2)]
    x2t = A.al(128, DM)
    xot = A.al(128, DM)
    mrow4 = A.al(1, DM)

    with nc.Block() as blk:
        @blk.sync
        def _(sync):
            s_x.dma(sync.dma_start(out=mrow4[:, :], in_=bo_d[:, :]))
        l1 = s_x.n

        @blk.tensor
        def _(te):
            te.wait_ge(s_x.h, l1)
            m = None
            for i in range(4):
                m = te.matmul(pb[i][:, :], ones_r[:, :],
                              mrow4[:, i * 512:(i + 1) * 512],
                              start=True, stop=True)
            s_mm.inc(m)
        m1 = s_mm.n

        @blk.vector
        def _(ve):
            ve.wait_ge(s_mm.h, m1)
            for i in range(4):
                ve.tensor_copy(bo_bc[:, i * 512:(i + 1) * 512], pb[i][:, :])
            s_ve.inc(ve.engine_nop())
        v1 = s_ve.n

        @blk.sync
        def _(sync):
            sync.wait_ge(s_ve.h, v1)
            s_x.dma(sync.dma_start(out=mrow4[:, :],
                                   in_=mod_d[:, 2 * DM:3 * DM]))
        l2 = s_x.n

        @blk.tensor
        def _(te):
            te.wait_ge(s_x.h, l2)
            m = None
            for i in range(4):
                m = te.matmul(pb[4 + i][:, :], ones_r[:, :],
                              mrow4[:, i * 512:(i + 1) * 512],
                              start=True, stop=True)
            s_mm.inc(m)
        m2 = s_mm.n

        @blk.vector
        def _(ve):
            ve.wait_ge(s_mm.h, m2)
            for i in range(4):
                ve.tensor_copy(ga_bc[:, i * 512:(i + 1) * 512],
                               pb[4 + i][:, :])
            s_ve.inc(ve.engine_nop())

    for q in range(NQT):
        with nc.Block() as blk:
            @blk.sync
            def _(sync, q=q):
                s_x.dma(sync.dma_start(out=xot[:, :],
                                       in_=x_d[q * 128:(q + 1) * 128, :]))
            xomk = s_x.n

            womk = []
            for fc in range(16):
                @blk.tensor
                def _(te, q=q, fc=fc):
                    te.wait_ge(s_ve.h, s_ve.n)
                    m = te.matmul(pb[4 + fc % 2][:, 0:128],
                                  attnacc[q][:, fc * 130:fc * 130 + 128],
                                  ident[:, :], is_transpose=True,
                                  start=True, stop=True)
                    s_mm.inc(m)
                tmk = s_mm.n

                @blk.vector
                def _(ve, fc=fc, tmk=tmk):
                    ve.wait_ge(s_mm.h, tmk)
                    ve.tensor_copy(afm[fc % 2][:, :], pb[4 + fc % 2][:, 0:128])
                    s_ve.inc(ve.engine_nop())
                emk = s_ve.n

                for gg in range(4):
                    @blk.scalar
                    def _(sc, fc=fc, gg=gg):
                        idx = fc * 4 + gg
                        if idx >= 2:
                            sc.wait_ge(s_mm.h, womk[idx - 2])
                        s_w.dma(sc.dma_start(
                            out=wostr[idx % 2][:, :],
                            in_=wo_d[fc * 128:(fc + 1) * 128,
                                     gg * 512:(gg + 1) * 512]))
                    wmk = s_w.n

                    @blk.tensor
                    def _(te, fc=fc, gg=gg, wmk=wmk, emk=emk):
                        te.wait_ge(s_w.h, wmk)
                        te.wait_ge(s_ve.h, emk)
                        m = te.matmul(pb[gg][:, :], afm[fc % 2][:, :],
                                      wostr[(fc * 4 + gg) % 2][:, :],
                                      start=(fc == 0), stop=(fc == 15))
                        s_mm.inc(m)
                    womk.append(s_mm.n)

            @blk.vector
            def _(ve, xomk=xomk):
                ve.wait_ge(s_mm.h, womk[-1])
                ve.wait_ge(s_x.h, xomk)
                for gg in range(4):
                    sl = slice(gg * 512, (gg + 1) * 512)
                    ve.tensor_tensor(x2t[:, sl], pb[gg][:, :], bo_bc[:, sl],
                                     ALU.add)
                    ve.tensor_tensor(x2t[:, sl], x2t[:, sl], ga_bc[:, sl],
                                     ALU.mult)
                    ve.tensor_tensor(x2t[:, sl], x2t[:, sl], xot[:, sl],
                                     ALU.add)
                s_ve.inc(ve.engine_nop())
            x2mk = s_ve.n

            @blk.gpsimd
            def _(gp, q=q, x2mk=x2mk):
                gp.wait_ge(s_ve.h, x2mk)
                s_sp.dma(gp.dma_start(out=x2_d[q * 128:(q + 1) * 128, :],
                                      in_=x2t[:, :]))
                gp.wait_ge(s_sp.h, s_sp.n)

    A.pop()
    A.pop()
    if stop_after == 'P4a':
        return _finish()

    A.push()
    h2hi = [A.al(128, OWN // 2, BF16) for _ in range(16)]
    h2lo = [A.al(128, OWN // 2, BF16) for _ in range(16)]
    A.push()
    cb1_bc = A.al(128, DM)
    sb_bc = A.al(128, DM)
    x2t = A.al(128, DM)
    h2t = A.al(128, DM)
    sct2 = A.al(128, 8)
    lgt = A.al(128, 64)
    mrow5 = A.al(1, 2 * DM)

    with nc.Block() as blk:
        @blk.sync
        def _(sync):
            s_x.dma(sync.dma_start(out=mrow5[:, 0:DM],
                                   in_=mod_d[:, 4 * DM:5 * DM]))
            s_x.dma(sync.dma_start(out=mrow5[:, DM:2 * DM],
                                   in_=mod_d[:, 3 * DM:4 * DM]))
        lm = s_x.n

        @blk.tensor
        def _(te):
            te.wait_ge(s_x.h, lm)
            m = None
            for i in range(4):
                te.matmul(pb[i][:, :], ones_r[:, :],
                          mrow5[:, i * 512:(i + 1) * 512],
                          start=True, stop=True)
                m = te.matmul(pb[4 + i][:, :], ones_r[:, :],
                              mrow5[:, DM + i * 512:DM + (i + 1) * 512],
                              start=True, stop=True)
            s_mm.inc(m)
        bm = s_mm.n

        @blk.vector
        def _(ve):
            ve.wait_ge(s_mm.h, bm)
            for i in range(4):
                ve.tensor_copy(cb1_bc[:, i * 512:(i + 1) * 512], pb[i][:, :])
                ve.tensor_copy(sb_bc[:, i * 512:(i + 1) * 512],
                               pb[4 + i][:, :])
            s_ve.inc(ve.engine_nop())

    for q in range(NQT):
        with nc.Block() as blk:
            @blk.sync
            def _(sync, q=q):
                s_x.dma(sync.dma_start(out=x2t[:, :],
                                       in_=x2_d[q * 128:(q + 1) * 128, :]))
            xmk = s_x.n

            @blk.vector
            def _(ve, xmk=xmk):
                ve.wait_ge(s_x.h, xmk)
                ve.reduce_sum(sct2[:, 0:1], x2t[:, :], AX.X)
                s_ve.inc(ve.engine_nop())
            r0mk = s_ve.n

            @blk.scalar
            def _(sc, xmk=xmk, r0mk=r0mk):
                sc.wait_ge(s_x.h, xmk)
                sc.wait_ge(s_ve.h, r0mk)
                s_ac.inc(sc.activation(h2t[:, :], x2t[:, :], AF.Square,
                                       accum_out=sct2[:, 1:2]))
            sqm = s_ac.n

            @blk.vector
            def _(ve, sqm=sqm):
                ve.wait_ge(s_ac.h, sqm)
                ve.tensor_single_scalar(sct2[:, 2:3], sct2[:, 0:1], 1.0 / DM,
                                        ALU.mult)
                ve.drain()
                ve.tensor_tensor(sct2[:, 3:4], sct2[:, 2:3], sct2[:, 2:3],
                                 ALU.mult)
                ve.tensor_single_scalar(sct2[:, 4:5], sct2[:, 1:2], 1.0 / DM,
                                        ALU.mult)
                ve.drain()
                ve.tensor_tensor(sct2[:, 4:5], sct2[:, 4:5], sct2[:, 3:4],
                                 ALU.subtract)
                ve.drain()
                s_ve.inc(ve.engine_nop())
            stm = s_ve.n

            @blk.scalar
            def _(sc, stm=stm):
                sc.wait_ge(s_ve.h, stm)
                s_ac.inc(sc.activation(sct2[:, 5:6], sct2[:, 4:5], AF.Sqrt,
                                       bias=epsc[:, 0:1]))
            sdm = s_ac.n

            @blk.vector
            def _(ve, sdm=sdm):
                ve.wait_ge(s_ac.h, sdm)
                ve.reciprocal(sct2[:, 6:7], sct2[:, 5:6])
                ve.drain()
                ve.tensor_scalar(h2t[:, :], x2t[:, :], sct2[:, 2:3],
                                 sct2[:, 6:7], ALU.subtract, ALU.mult)
                ve.tensor_tensor(h2t[:, :], h2t[:, :], cb1_bc[:, :], ALU.mult)
                ve.tensor_tensor(h2t[:, :], h2t[:, :], sb_bc[:, :], ALU.add)
                s_ve.inc(ve.engine_nop())
            h2mk = s_ve.n

            for half in range(2):
                @blk.tensor
                def _(te, half=half, h2mk=h2mk):
                    te.wait_ge(s_ve.h, h2mk if half == 0 else s_ve.n)
                    m = None
                    for j in range(8):
                        dc = half * 8 + j
                        m = te.matmul(pb[j][:, 0:128],
                                      h2t[:, dc * 128:(dc + 1) * 128],
                                      ident[:, :], is_transpose=True,
                                      start=True, stop=True)
                    s_mm.inc(m)
                tmk = s_mm.n

                @blk.vector
                def _(ve, half=half, q=q, tmk=tmk):
                    ve.wait_ge(s_mm.h, tmk)
                    sl = slice(q * 128, (q + 1) * 128)
                    for j in range(8):
                        dc = half * 8 + j
                        ve.tensor_copy(h2hi[dc][:, sl], pb[j][:, 0:128])
                    for j in range(8):
                        dc = half * 8 + j
                        ve.tensor_tensor(h2lo[dc][:, sl], pb[j][:, 0:128],
                                         h2hi[dc][:, sl], ALU.subtract)
                    s_ve.inc(ve.engine_nop())
            trh2 = s_ve.n

            @blk.tensor
            def _(te, q=q, trh2=trh2):
                te.wait_ge(s_ve.h, trh2)
                m = None
                sl = slice(q * 128, (q + 1) * 128)
                for dc in range(16):
                    for a, b in ((h2hi, rwTh), (h2hi, rwTl), (h2lo, rwTh)):
                        m = te.matmul(pb[6][:, 0:16], a[dc][:, sl],
                                      b[:, dc * 16:(dc + 1) * 16],
                                      start=(dc == 0 and a is h2hi
                                             and b is rwTh),
                                      stop=(dc == 15 and a is h2lo))
                s_mm.inc(m)
            lgmk = s_mm.n

            @blk.vector
            def _(ve, lgmk=lgmk):
                ve.wait_ge(s_mm.h, lgmk)
                lg = lgt[:, 0:16]
                lb = lgt[:, 16:32]
                ws = lgt[:, 48:64]
                m1 = sct2[:, 0:1]
                ve.tensor_copy(lg, pb[6][:, 0:16])
                ve.drain()
                ve.tensor_tensor(lb, lg, rb_bc[:, :], ALU.add)
                ve.drain()
                ve.reduce_max(m1, lb, AX.X)
                ve.drain()
                ve.tensor_single_scalar(ws, lb, m1, ALU.is_equal)
                ve.drain()
                ve.scalar_tensor_tensor(lb, ws, -1e30, lb, ALU.mult, ALU.add)
                ve.drain()
                ve.reduce_max(m1, lb, AX.X)
                ve.drain()
                ve.tensor_single_scalar(lb, lb, m1, ALU.is_equal)
                ve.drain()
                ve.tensor_tensor(ws, ws, lb, ALU.add)
                ve.drain()
                ve.reduce_max(m1, lg, AX.X, negate=True)
                ve.drain()
                s_ve.inc(ve.engine_nop())
            r1mk = s_ve.n

            @blk.scalar
            def _(sc, r1mk=r1mk):
                sc.wait_ge(s_ve.h, r1mk)
                s_ac.inc(sc.activation(lgt[:, 32:48], lgt[:, 0:16], AF.Exp,
                                       bias=sct2[:, 0:1]))
            exmk = s_ac.n

            if lg_d is not None:
                @blk.gpsimd
                def _(gp, q=q, exmk=exmk):
                    gp.wait_ge(s_ac.h, exmk)
                    s_sp.dma(gp.dma_start(
                        out=lg_d[q * 128:(q + 1) * 128, :],
                        in_=lgt[:, 0:16]))
                    gp.wait_ge(s_sp.h, s_sp.n)

            @blk.vector
            def _(ve, q=q, exmk=exmk):
                ve.wait_ge(s_ac.h, exmk)
                ex = lgt[:, 32:48]
                ws = lgt[:, 48:64]
                m1 = sct2[:, 0:1]
                ve.reduce_sum(m1, ex, AX.X)
                ve.drain()
                ve.reciprocal(m1, m1)
                ve.drain()
                ve.tensor_single_scalar(ex, ex, m1, ALU.mult)
                ve.drain()
                ve.tensor_tensor(ws, ex, ws, ALU.mult)
                ve.drain()
                ve.reduce_sum(m1, ws, AX.X)
                ve.drain()
                ve.reciprocal(m1, m1)
                ve.drain()
                ve.tensor_single_scalar(combine[:, q * 16:(q + 1) * 16],
                                        ws, m1, ALU.mult)
                s_ve.inc(ve.engine_nop())

    with nc.Block() as blk:
        if cmb_d is not None:
            @blk.gpsimd
            def _(gp):
                for q in range(NQT):
                    s_sp.dma(gp.dma_start(
                        out=cmb_d[q * 128:(q + 1) * 128, :],
                        in_=combine[:, q * 16:(q + 1) * 16]))
                gp.wait_ge(s_sp.h, s_sp.n)
        if h2t_d is not None:
            @blk.gpsimd
            def _(gp):
                for dc in range(16):
                    s_sp.dma(gp.dma_start(
                        out=h2t_d[dc * 128:(dc + 1) * 128, :],
                        in_=h2_fm[dc][:, :]))
                gp.wait_ge(s_sp.h, s_sp.n)

    A.pop()
    if stop_after == 'P4b':
        return _finish()

    A.push()
    moe = [A.al(128, DM) for _ in range(NQT)]
    A.push()
    hidh = [A.al(128, 256, BF16) for _ in range(16)]
    hidl = [A.al(128, 256, BF16) for _ in range(16)]
    ts4 = [A.al(128, 512) for _ in range(4)]
    wsa = [A.al(128, 256) for _ in range(2)]
    wsb = [A.al(128, 256) for _ in range(2)]
    wah = [A.al(128, 128, BF16) for _ in range(2)]
    wal = [A.al(128, 128, BF16) for _ in range(2)]
    wbh = [A.al(128, 128, BF16) for _ in range(2)]
    wbl = [A.al(128, 128, BF16) for _ in range(2)]
    wdt = [A.al(128, 512) for _ in range(2)]
    wdh = [A.al(128, 256, BF16) for _ in range(2)]
    wdl = [A.al(128, 256, BF16) for _ in range(2)]

    def emit_ug(blk, src_u, src_g, hf_cols, n_i2, qh_list, e=None):
        pass

    for qh in range(2):
        with nc.Block() as blk:
            split_mark = [s_ve.n, s_ve.n]
            mm_mark2 = [s_mm.n, s_mm.n]
            for hfp in range(8):
                umrk = []
                for dc in range(16):
                    @blk.scalar
                    def _(sc, hfp=hfp, dc=dc, sm=split_mark[dc % 2]):
                        sc.wait_ge(s_ve.h, sm)
                        s_w.dma(sc.dma_start(
                            out=wsa[dc % 2][:, :],
                            in_=su_d[dc * 128:(dc + 1) * 128,
                                     hfp * 256:(hfp + 1) * 256]))
                        s_w.dma(sc.dma_start(
                            out=wsb[dc % 2][:, :],
                            in_=sg_d[dc * 128:(dc + 1) * 128,
                                     hfp * 256:(hfp + 1) * 256]))
                    wmk = s_w.n

                    @blk.vector
                    def _(ve, dc=dc, wmk=wmk, mm2=mm_mark2[dc % 2]):
                        ve.wait_ge(s_w.h, wmk)
                        ve.wait_ge(s_mm.h, mm2)
                        i = dc % 2
                        ve.tensor_copy(wah[i][:, :], wsa[i][:, :])
                        ve.tensor_copy(wbh[i][:, :], wsb[i][:, :])
                        ve.tensor_tensor(wal[i][:, :], wsa[i][:, :],
                                         wah[i][:, :], ALU.subtract)
                        ve.tensor_tensor(wbl[i][:, :], wsb[i][:, :],
                                         wbh[i][:, :], ALU.subtract)
                        s_ve.inc(ve.engine_nop())
                    split_mark[dc % 2] = s_ve.n
                    spmk = s_ve.n

                    @blk.tensor
                    def _(te, qh=qh, dc=dc, spmk=spmk):
                        te.wait_ge(s_ve.h, spmk)
                        if dc == 0:
                            te.wait_ge(s_ac.h, s_ac.n)
                        i = dc % 2
                        m = None
                        for i2 in range(2):
                            csl = slice(i2 * 128, (i2 + 1) * 128)
                            hsl = slice(qh * 512, (qh + 1) * 512)
                            for wgt, act, first, last in (
                                    (wah[i], h2hi, dc == 0, False),
                                    (wal[i], h2hi, False, False),
                                    (wah[i], h2lo, False, dc == 15)):
                                m = te.matmul(pb[i2][:, :], wgt[:, csl],
                                              act[dc][:, hsl],
                                              start=first, stop=last)
                            for wgt, act, first, last in (
                                    (wbh[i], h2hi, dc == 0, False),
                                    (wbl[i], h2hi, False, False),
                                    (wbh[i], h2lo, False, dc == 15)):
                                m = te.matmul(pb[2 + i2][:, :], wgt[:, csl],
                                              act[dc][:, hsl],
                                              start=first, stop=last)
                        s_mm.inc(m)
                    umrk.append(s_mm.n)
                    mm_mark2[dc % 2] = s_mm.n

                @blk.scalar
                def _(sc, hfp=hfp, umrk=umrk):
                    sc.wait_ge(s_mm.h, umrk[-1])
                    for i2 in range(2):
                        s_ac.inc(sc.activation(ts4[i2][:, :],
                                               pb[i2][:, :], AF.Silu))
                simk = s_ac.n

                @blk.vector
                def _(ve, hfp=hfp, simk=simk):
                    ve.wait_ge(s_ac.h, simk)
                    for i2 in range(2):
                        hf = hfp * 2 + i2
                        ve.tensor_tensor(ts4[i2][:, :], ts4[i2][:, :],
                                         pb[2 + i2][:, :], ALU.mult)
                        ve.tensor_copy(hidh[hf][:, :], ts4[i2][:, :])
                        ve.drain()
                        ve.tensor_tensor(hidl[hf][:, :], ts4[i2][:, :],
                                         hidh[hf][:, :], ALU.subtract)
                    s_ve.inc(ve.engine_nop())

            sd_split = [s_ve.n, s_ve.n]
            sd_mm2 = [s_mm.n, s_mm.n]
            for fg in range(4):
                sdm = []
                for hfc in range(16):
                    @blk.scalar
                    def _(sc, fg=fg, hfc=hfc, sm=sd_split[hfc % 2]):
                        sc.wait_ge(s_ve.h, sm)
                        s_w.dma(sc.dma_start(
                            out=wdt[hfc % 2][:, :],
                            in_=sd_d[hfc * 128:(hfc + 1) * 128,
                                     fg * 512:(fg + 1) * 512]))
                    wmk = s_w.n

                    @blk.vector
                    def _(ve, hfc=hfc, wmk=wmk, mm2=sd_mm2[hfc % 2]):
                        ve.wait_ge(s_w.h, wmk)
                        ve.wait_ge(s_mm.h, mm2)
                        i = hfc % 2
                        ve.tensor_copy(wdh[i][:, :], wdt[i][:, :])
                        ve.drain()
                        ve.tensor_tensor(wdl[i][:, :], wdt[i][:, :],
                                         wdh[i][:, :], ALU.subtract)
                        s_ve.inc(ve.engine_nop())
                    sd_split[hfc % 2] = s_ve.n
                    spmk = s_ve.n

                    @blk.tensor
                    def _(te, qh=qh, fg=fg, hfc=hfc, spmk=spmk):
                        te.wait_ge(s_ve.h, spmk)
                        i = hfc % 2
                        m = None
                        for qq in range(4):
                            qsl = slice(qq * 128, (qq + 1) * 128)
                            for wgt, act in ((wdh[i], hidh), (wdl[i], hidh),
                                             (wdh[i], hidl)):
                                m = te.matmul(
                                    pb[4 + qq][:, :], act[hfc][:, qsl],
                                    wgt[:, :],
                                    start=(hfc == 0 and wgt is wdh[i]
                                           and act is hidh),
                                    stop=(hfc == 15 and act is hidl))
                        s_mm.inc(m)
                    sdm.append(s_mm.n)
                    sd_mm2[hfc % 2] = s_mm.n

                @blk.vector
                def _(ve, qh=qh, fg=fg, sdm=sdm):
                    ve.wait_ge(s_mm.h, sdm[-1])
                    for qq in range(4):
                        qidx = qh * 4 + qq
                        ve.tensor_copy(moe[qidx][:, fg * 512:(fg + 1) * 512],
                                       pb[4 + qq][:, :])
                    s_ve.inc(ve.engine_nop())

    for e in range(16):
        with nc.Block() as blk:
            split_mark = [s_ve.n, s_ve.n]
            mm_mark2 = [s_mm.n, s_mm.n]
            for qtr in range(4):
                umrk = []
                for dc in range(16):
                    @blk.scalar
                    def _(sc, e=e, qtr=qtr, dc=dc, sm=split_mark[dc % 2]):
                        sc.wait_ge(s_ve.h, sm)
                        s_w.dma(sc.dma_start(
                            out=wsa[dc % 2][:, :],
                            in_=wu_d[e][dc * 128:(dc + 1) * 128,
                                        qtr * 256:(qtr + 1) * 256]))
                        s_w.dma(sc.dma_start(
                            out=wsb[dc % 2][:, :],
                            in_=wg_d[e][dc * 128:(dc + 1) * 128,
                                        qtr * 256:(qtr + 1) * 256]))
                    wmk = s_w.n

                    @blk.vector
                    def _(ve, dc=dc, wmk=wmk, mm2=mm_mark2[dc % 2]):
                        ve.wait_ge(s_w.h, wmk)
                        ve.wait_ge(s_mm.h, mm2)
                        i = dc % 2
                        ve.tensor_copy(wah[i][:, :], wsa[i][:, :])
                        ve.tensor_copy(wbh[i][:, :], wsb[i][:, :])
                        ve.tensor_tensor(wal[i][:, :], wsa[i][:, :],
                                         wah[i][:, :], ALU.subtract)
                        ve.tensor_tensor(wbl[i][:, :], wsb[i][:, :],
                                         wbh[i][:, :], ALU.subtract)
                        s_ve.inc(ve.engine_nop())
                    split_mark[dc % 2] = s_ve.n
                    spmk = s_ve.n

                    @blk.tensor
                    def _(te, dc=dc, spmk=spmk):
                        te.wait_ge(s_ve.h, spmk)
                        if dc == 0:
                            te.wait_ge(s_ac.h, s_ac.n)
                        i = dc % 2
                        m = None
                        for i2 in range(2):
                            csl = slice(i2 * 128, (i2 + 1) * 128)
                            for qh in range(2):
                                hsl = slice(qh * 512, (qh + 1) * 512)
                                for wgt, act in ((wah[i], h2hi),
                                                 (wal[i], h2hi),
                                                 (wah[i], h2lo)):
                                    m = te.matmul(
                                        pb[i2 * 2 + qh][:, :], wgt[:, csl],
                                        act[dc][:, hsl],
                                        start=(dc == 0 and wgt is wah[i]
                                               and act is h2hi),
                                        stop=(dc == 15 and act is h2lo))
                                for wgt, act in ((wbh[i], h2hi),
                                                 (wbl[i], h2hi),
                                                 (wbh[i], h2lo)):
                                    m = te.matmul(
                                        pb[4 + i2 * 2 + qh][:, :],
                                        wgt[:, csl], act[dc][:, hsl],
                                        start=(dc == 0 and wgt is wbh[i]
                                               and act is h2hi),
                                        stop=(dc == 15 and act is h2lo))
                        s_mm.inc(m)
                    umrk.append(s_mm.n)
                    mm_mark2[dc % 2] = s_mm.n

                @blk.scalar
                def _(sc, qtr=qtr, umrk=umrk):
                    sc.wait_ge(s_mm.h, umrk[-1])
                    for i2 in range(2):
                        for qh in range(2):
                            s_ac.inc(sc.activation(ts4[i2 * 2 + qh][:, :],
                                                   pb[i2 * 2 + qh][:, :],
                                                   AF.Silu))
                simk = s_ac.n

                @blk.vector
                def _(ve, qtr=qtr, simk=simk):
                    ve.wait_ge(s_ac.h, simk)
                    for i2 in range(2):
                        for qh in range(2):
                            g = i2 * 2 + qh
                            hx = (qtr * 2 + i2) * 2 + qh
                            ve.tensor_tensor(ts4[g][:, :], ts4[g][:, :],
                                             pb[4 + g][:, :], ALU.mult)
                            ve.tensor_copy(hidh[hx][:, :], ts4[g][:, :])
                            ve.drain()
                            ve.tensor_tensor(hidl[hx][:, :], ts4[g][:, :],
                                             hidh[hx][:, :], ALU.subtract)
                    s_ve.inc(ve.engine_nop())

            wd_split = [s_ve.n, s_ve.n]
            wd_mm2 = [s_mm.n, s_mm.n]
            for fg in range(4):
                wdm = []
                for hfc in range(8):
                    @blk.scalar
                    def _(sc, e=e, fg=fg, hfc=hfc, sm=wd_split[hfc % 2]):
                        sc.wait_ge(s_ve.h, sm)
                        s_w.dma(sc.dma_start(
                            out=wdt[hfc % 2][:, :],
                            in_=wd_d[e][hfc * 128:(hfc + 1) * 128,
                                        fg * 512:(fg + 1) * 512]))
                    wmk = s_w.n

                    @blk.vector
                    def _(ve, hfc=hfc, wmk=wmk, mm2=wd_mm2[hfc % 2]):
                        ve.wait_ge(s_w.h, wmk)
                        ve.wait_ge(s_mm.h, mm2)
                        i = hfc % 2
                        ve.tensor_copy(wdh[i][:, :], wdt[i][:, :])
                        ve.drain()
                        ve.tensor_tensor(wdl[i][:, :], wdt[i][:, :],
                                         wdh[i][:, :], ALU.subtract)
                        s_ve.inc(ve.engine_nop())
                    wd_split[hfc % 2] = s_ve.n
                    spmk = s_ve.n

                    @blk.tensor
                    def _(te, fg=fg, hfc=hfc, spmk=spmk):
                        te.wait_ge(s_ve.h, spmk)
                        i = hfc % 2
                        m = None
                        for qq in range(8):
                            qh, ql = qq // 4, qq % 4
                            hx = hfc * 2 + qh
                            qsl = slice(ql * 128, (ql + 1) * 128)
                            for wgt, act in ((wdh[i], hidh), (wdl[i], hidh),
                                             (wdh[i], hidl)):
                                m = te.matmul(
                                    pb[qq][:, :], act[hx][:, qsl], wgt[:, :],
                                    start=(hfc == 0 and wgt is wdh[i]
                                           and act is hidh),
                                    stop=(hfc == 7 and act is hidl))
                        s_mm.inc(m)
                    wdm.append(s_mm.n)
                    wd_mm2[hfc % 2] = s_mm.n

                @blk.vector
                def _(ve, e=e, fg=fg, wdm=wdm):
                    ve.wait_ge(s_mm.h, wdm[-1])
                    for qq in range(8):
                        ve.scalar_tensor_tensor(
                            moe[qq][:, fg * 512:(fg + 1) * 512],
                            pb[qq][:, :], combine[:, qq * 16 + e:
                                                  qq * 16 + e + 1],
                            moe[qq][:, fg * 512:(fg + 1) * 512],
                            ALU.mult, ALU.add)
                    s_ve.inc(ve.engine_nop())

    A.pop()
    A.push()
    gb_bc = A.al(128, DM)
    x2r = A.al(128, DM)
    outt = A.al(128, DM)
    mrow6 = A.al(1, DM)

    with nc.Block() as blk:
        @blk.sync
        def _(sync):
            s_x.dma(sync.dma_start(out=mrow6[:, :],
                                   in_=mod_d[:, 5 * DM:6 * DM]))
        lm6 = s_x.n

        @blk.tensor
        def _(te):
            te.wait_ge(s_x.h, lm6)
            m = None
            for i in range(4):
                m = te.matmul(pb[i][:, :], ones_r[:, :],
                              mrow6[:, i * 512:(i + 1) * 512],
                              start=True, stop=True)
            s_mm.inc(m)
        bm6 = s_mm.n

        @blk.vector
        def _(ve):
            ve.wait_ge(s_mm.h, bm6)
            for i in range(4):
                ve.tensor_copy(gb_bc[:, i * 512:(i + 1) * 512], pb[i][:, :])
            s_ve.inc(ve.engine_nop())

    with nc.Block() as blk:
        prev_out = s_sp.n
        for q in range(NQT):
            @blk.sync
            def _(sync, q=q):
                sync.wait_ge(s_ve.h, s_ve.n)
                s_x.dma(sync.dma_start(out=x2r[:, :],
                                       in_=x2_d[q * 128:(q + 1) * 128, :]))
            xmk = s_x.n

            @blk.vector
            def _(ve, q=q, xmk=xmk, prev_out=prev_out):
                ve.wait_ge(s_x.h, xmk)
                ve.wait_ge(s_sp.h, prev_out)
                ve.tensor_tensor(outt[:, :], moe[q][:, :], gb_bc[:, :],
                                 ALU.mult)
                ve.tensor_tensor(outt[:, :], outt[:, :], x2r[:, :], ALU.add)
                s_ve.inc(ve.engine_nop())
            omk = s_ve.n

            @blk.gpsimd
            def _(gp, q=q, omk=omk):
                gp.wait_ge(s_ve.h, omk)
                s_sp.dma(gp.dma_start(out=out_d[q * 128:(q + 1) * 128, :],
                                      in_=outt[:, :]))
            prev_out = s_sp.n

        @blk.gpsimd
        def _(gp):
            gp.wait_ge(s_sp.h, s_sp.n)

    return nc


def kernel(**inputs):
    x = np.ascontiguousarray(np.asarray(inputs["L0_attn_x"],
                                        dtype=np.float32))[0]
    assert x.shape == (S, DM)
    nc = build_nc()
    ar = np.arange(S, dtype=np.float32)

    base = {
        "wq": np.ascontiguousarray(inputs["L0_attn_Wq"], dtype=np.float32),
        "wkv": np.ascontiguousarray(inputs["L0_attn_Wkv"], dtype=np.float32),
        "wo": np.ascontiguousarray(inputs["L0_attn_Wo"], dtype=np.float32),
        "bo": np.ascontiguousarray(inputs["L0_attn_bo"],
                                   dtype=np.float32).reshape(1, DM),
        "rw": np.ascontiguousarray(inputs["L0_ffn_router_w"],
                                   dtype=np.float32),
        "rb": np.ascontiguousarray(inputs["L0_ffn_router_bias"],
                                   dtype=np.float32).reshape(1, 16),
        "wu": np.ascontiguousarray(inputs["L0_ffn_Wu"], dtype=np.float32),
        "wg": np.ascontiguousarray(inputs["L0_ffn_Wg"], dtype=np.float32),
        "wd": np.ascontiguousarray(inputs["L0_ffn_Wd"], dtype=np.float32),
        "su": np.ascontiguousarray(inputs["L0_ffn_Su"],
                                   dtype=np.float32).reshape(DM, DM),
        "sg": np.ascontiguousarray(inputs["L0_ffn_Sg"],
                                   dtype=np.float32).reshape(DM, DM),
        "sd": np.ascontiguousarray(inputs["L0_ffn_Sd"],
                                   dtype=np.float32).reshape(DM, DM),
        "cvec": np.ascontiguousarray(inputs["L0_c"],
                                     dtype=np.float32).reshape(1, DM),
        "wmod": np.ascontiguousarray(inputs["L0_wmod"], dtype=np.float32),
    }
    in_maps = []
    for c in range(8):
        sl = slice(c * OWN, (c + 1) * OWN)
        perm_x = np.concatenate(
            [x[sl], x[:c * OWN], x[(c + 1) * OWN:]], axis=0)
        perm_p = np.concatenate([ar[sl], ar[:c * OWN], ar[(c + 1) * OWN:]])
        m = dict(base)
        m["x"] = np.ascontiguousarray(perm_x)
        m["tokr"] = np.ascontiguousarray(perm_p.reshape(1, S))
        m["tokc"] = np.ascontiguousarray(perm_p.reshape(64, 128).T)
        in_maps.append(m)

    res = run_bass_kernel_spmd(nc, in_maps, core_ids=list(range(8)))
    out = np.concatenate([np.asarray(r["out"]) for r in res.results], axis=0)
    return out.reshape(1, S, DM).astype(np.float32)
